# version 01
# baseline (speedup 1.0000x reference)
import numpy as np
import ml_dtypes
import concourse.bass as bass
import concourse.mybir as mybir
from concourse.ap import AP as _AP
from concourse.bass_utils import run_bass_kernel_spmd
from contextlib import ExitStack

F32 = mybir.dt.float32
F32R = mybir.dt.float32r
BF16 = mybir.dt.bfloat16
AF = mybir.ActivationFunctionType
ALU = mybir.AluOpType

NCORES = 8
B_LOC = 2
T = 2048
D = 1024
TOK = B_LOC * T
NH = 16
HD = 64
CH = 64
NCH = T // CH
FF = 4096
NE = 8
EPS_LN = 1e-5
PAD = -30000.0
SHARED_SCALE = 0.1

ALL_PHASES = ("p1", "p2", "p3", "p4")
P3_BARRIER = 0
P3_DEBUG = None
P1_PARTS = frozenset(("v", "wk", "pairs", "smt", "rd", "u", "proj"))
P1_LIMIT = None


class Ctx:

    def __init__(self, nc, ectx, self_sync=False):
        self.self_sync = self_sync
        self.nc = nc
        self.ectx = ectx
        self.n = {}
        self.sems = {}
        self.last_wait = {}
        self.engs = {"t": nc.tensor, "v": nc.vector, "s": nc.scalar,
                     "g": nc.gpsimd, "q": nc.sync}
        for s in ("t", "v", "s", "g"):
            self.sems[s] = ectx.enter_context(nc.semaphore(s))
            self.n[s] = 0
        self.streams = []

    def inc(self, inst, name, amt=1):
        self.n[name] += amt
        inst.then_inc(self.sems[name], amt)
        if self.self_sync and name in ("t", "v", "s", "g"):
            self.engs[name].wait_ge(self.sems[name], self.n[name])
            self.last_wait[(name, name)] = self.n[name]
        return self.n[name]

    def t(self, inst):
        return self.inc(inst, "t")

    def v(self, inst):
        return self.inc(inst, "v")

    def s(self, inst):
        return self.inc(inst, "s")

    def g(self, inst):
        return self.inc(inst, "g")

    def stream(self, name):
        if name not in self.sems:
            self.sems[name] = self.ectx.enter_context(self.nc.semaphore(name))
            self.n[name] = 0
            self.streams.append(name)
        return name

    def dma(self, q, stream, dst, src, deps=()):
        ename = {"qs": "q", "qg": "g", "qa": "s"}[q]
        for sname, val in deps:
            self.wait(ename, sname, val)
        eng = {"qs": self.nc.sync, "qg": self.nc.gpsimd, "qa": self.nc.scalar}[q]
        inst = eng.dma_start(out=dst, in_=src)
        return self.inc(inst, self.stream(stream), 16)

    def wait(self, ename, sname, val):
        if val <= 0:
            return
        key = (ename, sname)
        if self.last_wait.get(key, -1) >= val:
            return
        self.last_wait[key] = val
        self.engs[ename].wait_ge(self.sems[sname], val)

    def barrier(self):
        for e in ("t", "v", "s", "g", "q"):
            for sname in ("t", "v", "s", "g", *self.streams):
                ename_of = {"t": "t", "v": "v", "s": "s", "g": "g"}.get(sname)
                if ename_of == e:
                    continue
                self.wait(e, sname, self.n[sname])

    def sbuf(self, name, shape, dt=F32, scope=None):
        return (scope or self.ectx).enter_context(
            self.nc.sbuf_tensor(name, shape, dt))

    def psum(self, name, shape, dt=F32, scope=None):
        return (scope or self.ectx).enter_context(
            self.nc.psum_tensor(name, shape, dt))


def build_program(phases=ALL_PHASES, dev=False):
    nc = bass.Bass("TRN2", target_bir_lowering=False,
                   detect_race_conditions=bool(dev))
    dram = {}

    def din(name, shape, dt=F32):
        dram[name] = nc.dram_tensor(name, shape, dt, kind="ExternalInput").ap()

    din("x", [D, TOK])
    din("WwT", [D, D]); din("WrT", [D, D])
    din("decay", [NH]); din("log_alpha", [NH])
    din("Ctab", [D, D]); din("Stab", [D, D])
    din("wfr", [D, D]); din("wfi", [D, D]); din("fbias2d", [128, 8])
    din("qkw", [D, NH * 2 * HD]); din("vw", [D, NH * HD]); din("ow", [NH * HD, D])
    din("rpeT", [T, NH])
    din("Wg", [D, NE])
    din("W1", [NE, D, FF]); din("b1_2d", [128, 32 * NE])
    din("W2", [NE, FF, D])
    din("Ws1", [D, FF]); din("bs1_2d", [128, 32]); din("Ws2", [FF, D])
    din("b2s", [NE + 1, D]); din("c01", [1, 1024])
    out = nc.dram_tensor("out", [D, TOK], F32, kind="ExternalOutput").ap()

    scr = dict(
        scrA=nc.dram_tensor("scrA", [D, D], BF16, kind="Internal").ap(),
        scrB=nc.dram_tensor("scrB", [D, D], BF16, kind="Internal").ap(),
        cumpad=nc.dram_tensor("cumpad", [NH, 4096], F32, kind="Internal").ap(),
        xnscr=nc.dram_tensor("xnscr", [D, TOK], BF16, kind="Internal").ap(),
        xs1=nc.dram_tensor("xs1", [D, TOK], F32, kind="Internal").ap(),
        xs2=nc.dram_tensor("xs2", [D, TOK], F32, kind="Internal").ap(),
    )

    with ExitStack() as ectx:
        c = Ctx(nc, ectx, self_sync=bool(dev))
        Builder(c, dram, out, scr, phases).run()
    return nc


class Builder:
    def __init__(self, c, dram, out, scr, phases):
        self.c = c
        self.nc = c.nc
        self.dram = dram
        self.out = out
        self.scr = scr
        self.phases = phases
        self.x_dep = {}

    def run(self):
        c, nc = self.c, self.nc
        self.ones_row = c.sbuf("ones_row", [1, 128], F32)
        self.ones_col = c.sbuf("ones_col", [128, 1], F32)
        self.eye128 = c.sbuf("eye", [128, 128], F32)
        self.rep = c.sbuf("rep", [128, 2 * 512], F32)
        self.ln_tmp = c.sbuf("ln_tmp", [128, 3 * 512], F32)
        self.xn_tmp = c.sbuf("xn_tmp", [128, 512], F32)

        self.consts = c.sbuf("consts", [128, 4], F32)
        c.g(nc.gpsimd.memset(self.consts[:, 0:1], EPS_LN))
        c.g(nc.gpsimd.memset(self.consts[:, 1:2], 0.125))
        c.g(nc.gpsimd.memset(self.consts[:, 2:3], 128.0))
        c.g(nc.gpsimd.memset(self.ones_row[:], 1.0))
        self.consts_g = c.g(nc.gpsimd.memset(self.ones_col[:], 1.0))
        self.ones128 = c.sbuf("ones128", [128, 128], F32)
        self.x_scope = ExitStack()
        self.x_sb = c.sbuf("x_sb", [128, 8 * TOK], F32, scope=self.x_scope)
        gv = c.g(nc.gpsimd.memset(self.ones128[:], 1.0))
        c.wait("g", "g", gv)
        self.eye_ready = c.g(nc.gpsimd.affine_select(
            out=self.eye128[:], in_=self.ones128[:], compare_op=ALU.is_equal,
            fill=0.0, base=0, pattern=[[-1, 128]], channel_multiplier=1))

        for dc in range(8):
            c.dma("qs", "sx", self.xch(dc),
                  self.dram["x"][dc * 128:(dc + 1) * 128, :])
        self.x_dep = {"sx": c.n["sx"]}

        if "p1" in self.phases:
            self.p1_ttt()
            c.barrier()
        if "p2" in self.phases:
            self.build_M()
            c.barrier()
        if "p2" in self.phases:
            self.p2_ffn()
            c.barrier()
        if "p3" in self.phases:
            self.p3_attn()
            c.barrier()
        if "p4" in self.phases:
            self.p4_moe()
        else:
            self.store_x_to_out()
        self.final_sync()

    def xch(self, dc, sl=None):
        sl = sl or slice(0, TOK)
        return self.x_sb[:, dc * TOK + sl.start: dc * TOK + sl.stop]

    def wait_x(self, ename):
        for sname, val in self.x_dep.items():
            self.c.wait(ename, sname, val)

    def store_x_to_out(self):
        c = self.c
        self.wait_x("q")
        if "p3" in self.phases:
            if P3_DEBUG is None:
                for dc in range(8):
                    c.dma("qs", "so", self.out[dc * 128:(dc + 1) * 128, :],
                          self.scr["xs2"][dc * 128:(dc + 1) * 128, :])
            c.wait("q", "so", c.n["so"])
        elif False:
            pass
        else:
            for dc in range(8):
                c.dma("qs", "so", self.out[dc * 128:(dc + 1) * 128, :],
                      self.xch(dc))

    def final_sync(self):
        c = self.c
        for sname in c.streams:
            c.wait("q", sname, c.n[sname])

    def ln_group(self, g, p_s, p_q, state, xsrc=None):
        c, nc = self.c, self.nc
        sl = slice(g * 512, (g + 1) * 512)
        if xsrc is None:
            xsrc = lambda dc: self.xch(dc, sl)
            self.wait_x("t")
            self.wait_x("s")
        c.wait("t", "g", self.consts_g)
        c.wait("t", "v", state.get("psum_free", 0))
        for dc in range(8):
            nc.tensor.matmul(p_s[:], self.ones128[:], xsrc(dc),
                             start=(dc == 0), stop=(dc == 7))
        sq = self.ln_tmp
        c.wait("s", "v", state.get("stat_v", 0))
        for dc in range(8):
            slot = dc % 2
            c.wait("s", "t", state.get(("sq", slot), 0))
            sv = c.s(nc.scalar.activation(
                sq[:, slot * 512:(slot + 1) * 512], xsrc(dc),
                AF.Square, bias=0.0, scale=1.0))
            c.wait("t", "s", sv)
            tv = c.inc(nc.tensor.matmul(p_q[:], self.ones128[:],
                                        sq[:, slot * 512:(slot + 1) * 512],
                                        start=(dc == 0), stop=(dc == 7)), "t")
            state[("sq", slot)] = tv
        tA = self.ln_tmp[:, 0:512]
        tB = self.ln_tmp[:, 512:1024]
        tC = self.ln_tmp[:, 1024:1536]
        c.wait("v", "t", tv)
        c.wait("v", "v", state.get("rep_free", 0))
        c.v(nc.vector.tensor_scalar_mul(tA, p_s[:], 1.0 / D))
        c.v(nc.vector.tensor_scalar_mul(tB, p_q[:], 1.0 / D))
        state["psum_free"] = c.n["v"]
        c.v(nc.vector.tensor_mul(tC, tA, tA))
        vv = c.v(nc.vector.tensor_sub(tB, tB, tC))
        c.wait("s", "v", vv)
        c.wait("s", "g", self.consts_g)
        sv = c.s(nc.scalar.activation(tC, tB, AF.Sqrt,
                                      bias=self.consts[:, 0:1], scale=1.0))
        c.wait("v", "s", sv)
        c.v(nc.vector.reciprocal(self.rep[:, 0:512], tC))
        vv = c.v(nc.vector.tensor_mul(self.rep[:, 512:1024], tA,
                                      self.rep[:, 0:512]))
        state["stat_v"] = vv
        return vv

    def xn_chunk(self, dst, dc, sl, rep_v, xsrc=None):
        c, nc = self.c, self.nc
        n = sl.stop - sl.start
        x_ap = self.xch(dc, sl) if xsrc is None else xsrc(dc)
        c.wait("v", "v", rep_v)
        c.v(nc.vector.tensor_mul(self.xn_tmp[:, 0:n], x_ap,
                                 self.rep[:, 0:n]))
        return c.v(nc.vector.tensor_sub(
            dst, self.xn_tmp[:, 0:n], self.rep[:, 512:512 + n]))

    def build_M(self):
        c, nc = self.c, self.nc
        dram = self.dram
        scrA, scrB = self.scr["scrA"], self.scr["scrB"]
        self.m_scope = ExitStack()
        self.M_sb = c.sbuf("M_sb", [128, 8 * D], BF16, scope=self.m_scope)
        with ExitStack() as sc:
            cs = c.sbuf("cs_sl", [128, 8 * 512], BF16, scope=sc)
            sst = c.sbuf("ss_sl", [128, 8 * 512], BF16, scope=sc)
            ssn = c.sbuf("ssn_sl", [128, 8 * 512], BF16, scope=sc)
            wr_sl = c.sbuf("wr_sl", [128, 8 * 128], BF16, scope=sc)
            wi_sl = c.sbuf("wi_sl", [128, 8 * 128], BF16, scope=sc)
            ab_sl = c.sbuf("ab_sl", [128, 2 * 8 * 128], BF16, scope=sc)
            evac = c.sbuf("evac_m", [128, 2 * 512], BF16, scope=sc)
            pA = c.psum("pA", [128, 512], scope=sc)
            pB = c.psum("pB", [128, 512], scope=sc)

            def load_cs(col0):
                q = 0
                for lc in range(8):
                    c.dma("qg", "mcs", cs[:, lc * 512:(lc + 1) * 512],
                          dram["Ctab"][lc * 128:(lc + 1) * 128, col0:col0 + 512],
                          deps=[("t", c.n["t"])])
                    q = c.dma("qg", "mcs", sst[:, lc * 512:(lc + 1) * 512],
                              dram["Stab"][lc * 128:(lc + 1) * 128, col0:col0 + 512])
                c.wait("v", "mcs", q)
                vv = 0
                for lc in range(8):
                    vv = c.v(nc.vector.tensor_scalar_mul(
                        ssn[:, lc * 512:(lc + 1) * 512],
                        sst[:, lc * 512:(lc + 1) * 512], -1.0))
                return vv

            for jg in range(2):
                vneg = load_cs(jg * 512)
                for kc in range(8):
                    qw = 0
                    for lc in range(8):
                        c.dma("qg", "mw", wr_sl[:, lc * 128:(lc + 1) * 128],
                              dram["wfr"][lc * 128:(lc + 1) * 128,
                                          kc * 128:(kc + 1) * 128],
                              deps=[("t", c.n["t"])])
                        qw = c.dma("qg", "mw", wi_sl[:, lc * 128:(lc + 1) * 128],
                                   dram["wfi"][lc * 128:(lc + 1) * 128,
                                               kc * 128:(kc + 1) * 128])
                    c.wait("t", "mw", qw)
                    c.wait("t", "v", vneg)
                    for lc in range(8):
                        nc.tensor.matmul(pA[:], wr_sl[:, lc * 128:(lc + 1) * 128],
                                         cs[:, lc * 512:(lc + 1) * 512],
                                         start=(lc == 0), stop=False)
                    for lc in range(8):
                        nc.tensor.matmul(pA[:], wi_sl[:, lc * 128:(lc + 1) * 128],
                                         sst[:, lc * 512:(lc + 1) * 512],
                                         start=False, stop=(lc == 7))
                    for lc in range(8):
                        nc.tensor.matmul(pB[:], wi_sl[:, lc * 128:(lc + 1) * 128],
                                         cs[:, lc * 512:(lc + 1) * 512],
                                         start=(lc == 0), stop=False)
                    for lc in range(8):
                        mm = nc.tensor.matmul(pB[:], wr_sl[:, lc * 128:(lc + 1) * 128],
                                              ssn[:, lc * 512:(lc + 1) * 512],
                                              start=False, stop=(lc == 7))
                    tv = c.inc(mm, "t")
                    c.wait("v", "t", tv)
                    c.v(nc.vector.tensor_copy(evac[:, 0:512], pA[:]))
                    vv = c.v(nc.vector.tensor_copy(evac[:, 512:1024], pB[:]))
                    c.wait("t", "v", vv)
                    c.wait("q", "v", vv)
                    c.dma("qs", "mst", scrA[kc * 128:(kc + 1) * 128,
                                            jg * 512:(jg + 1) * 512], evac[:, 0:512])
                    qa = c.dma("qs", "mst", scrB[kc * 128:(kc + 1) * 128,
                                                 jg * 512:(jg + 1) * 512],
                               evac[:, 512:1024])
                    c.wait("v", "mst", qa)
            self._apass_q = c.n["mst"]
            self.M_ready = 0
            for mg in range(2):
                vneg = load_cs(mg * 512)
                for jc in range(8):
                    qw = 0
                    for kc in range(8):
                        c.dma("qg", "mab", ab_sl[:, kc * 128:(kc + 1) * 128],
                              scrA[kc * 128:(kc + 1) * 128, jc * 128:(jc + 1) * 128],
                              deps=[("t", c.n["t"]), ("mst", self._apass_q)])
                        qw = c.dma("qg", "mab",
                                   ab_sl[:, 1024 + kc * 128:1024 + (kc + 1) * 128],
                                   scrB[kc * 128:(kc + 1) * 128,
                                        jc * 128:(jc + 1) * 128])
                    c.wait("t", "mab", qw)
                    c.wait("t", "v", vneg)
                    for kc in range(8):
                        nc.tensor.matmul(pA[:], ab_sl[:, kc * 128:(kc + 1) * 128],
                                         cs[:, kc * 512:(kc + 1) * 512],
                                         start=(kc == 0), stop=False)
                    for kc in range(8):
                        mm = nc.tensor.matmul(
                            pA[:], ab_sl[:, 1024 + kc * 128:1024 + (kc + 1) * 128],
                            ssn[:, kc * 512:(kc + 1) * 512],
                            start=False, stop=(kc == 7))
                    tv = c.inc(mm, "t")
                    c.wait("v", "t", tv)
                    vv = c.v(nc.vector.tensor_scalar_mul(
                        self.M_sb[:, jc * D + mg * 512: jc * D + (mg + 1) * 512],
                        pA[:], 1.0 / D))
                    c.wait("t", "v", vv)
                    self.M_ready = vv

    def p1_ttt(self):
        c, nc = self.c, self.nc
        dram = self.dram
        CL = 128
        NCHD = T // CL
        SW = 513
        with ExitStack() as sc:
            slab = c.sbuf("slab", [128, 2 * 8 * SW], BF16, scope=sc)
            vbuf = c.sbuf("vbuf", [128, 1024], BF16, scope=sc)
            wkg = c.sbuf("wkg", [128, 1024], BF16, scope=sc)
            reads = c.sbuf("reads", [128, 8 * 512], BF16, scope=sc)
            U_tab = c.sbuf("U_tab", [128, 8 * 64], F32, scope=sc)
            WwT_bf = c.sbuf("WwT_bf", [128, 8 * 1024], BF16, scope=sc)
            wrt_s = c.sbuf("wrt_s", [128, 8 * 512], BF16, scope=sc)
            MT_tab = c.sbuf("MT_tab", [128, 16 * 128], BF16, scope=sc)
            gp2_tab = c.sbuf("gp2_tab", [128, 8 * 128], BF16, scope=sc)
            gw_tab = c.sbuf("gw_tab", [128, 16 * 64], BF16, scope=sc)
            lnpair = c.sbuf("lnpair", [128, 8], F32, scope=sc)
            alpha_pair = c.sbuf("alpha_pair", [128, 8], F32, scope=sc)
            gC_pair = c.sbuf("gC_pair", [128, 8], F32, scope=sc)
            eye_bf = c.sbuf("eye_bf", [128, 128], BF16, scope=sc)
            smt_sb = c.sbuf("smt_sb", [128, 256], BF16, scope=sc)
            rkg_sb = c.sbuf("rkg_sb", [128, 128], F32, scope=sc)
            pSt = c.psum("pSt", [128, 512], scope=sc)
            pU = pSt
            pRd = c.psum("pRd", [128, 512], scope=sc)
            pTrA = c.psum("pTrA", [128, 1024], BF16, scope=sc)
            pTrB = c.psum("pTrB", [128, 1024], BF16, scope=sc)
            pV = c.psum("pV", [128, 512], scope=sc)
            pPj = c.psum("pPj", [128, 512], scope=sc)
            p_s = c.psum("p_s1", [128, 512], scope=sc)
            p_q = c.psum("p_q1", [128, 512], scope=sc)

            for dc in range(8):
                c.dma("qg", "wws", WwT_bf[:, dc * 1024:(dc + 1) * 1024],
                      dram["WwT"][dc * 128:(dc + 1) * 128, :])
            q_w = c.n["wws"]
            tsc = ExitStack()
            lngam_bc = c.sbuf("lngam_bc", [128, 16], F32, scope=tsc)
            alpha_bc = c.sbuf("alpha_bc", [128, 16], F32, scope=tsc)
            grow = c.sbuf("grow", [1, 64], F32, scope=tsc)
            cidx = c.sbuf("cidx", [128, 128], F32, scope=tsc)
            widx = c.sbuf("widx", [128, 64], F32, scope=tsc)
            mdif = c.sbuf("mdif", [128, 128], F32, scope=tsc)
            c.dma("qs", "p1s", grow[0:1, 0:16],
                  dram["decay"].rearrange("(a k) -> a k", a=1))
            q_sm = c.dma("qs", "p1s", grow[0:1, 16:32],
                         dram["log_alpha"].rearrange("(a k) -> a k", a=1))
            c.wait("s", "p1s", q_sm)
            c.s(nc.scalar.activation(grow[0:1, 32:48], grow[0:1, 0:16],
                                     AF.Sigmoid, bias=0.0, scale=1.0))
            sv = c.s(nc.scalar.activation(grow[0:1, 48:64], grow[0:1, 32:48],
                                          AF.Ln, bias=0.0, scale=1.0))
            sv_a = c.s(nc.scalar.activation(grow[0:1, 16:32], grow[0:1, 16:32],
                                            AF.Exp, bias=0.0, scale=1.0))
            c.wait("t", "s", sv_a)
            c.wait("t", "g", self.consts_g)
            tv = c.inc(nc.tensor.matmul(p_s[:, 0:16], self.ones_row[:],
                                        grow[0:1, 48:64], start=True, stop=True), "t")
            tv = c.inc(nc.tensor.matmul(p_q[:, 0:16], self.ones_row[:],
                                        grow[0:1, 16:32], start=True, stop=True), "t")
            c.wait("v", "t", tv)
            c.v(nc.vector.tensor_copy(lngam_bc[:], p_s[:, 0:16]))
            vv = c.v(nc.vector.tensor_copy(alpha_bc[:], p_q[:, 0:16]))
            for j in range(8):
                c.v(nc.vector.tensor_copy(lnpair[0:64, j:j + 1],
                                          lngam_bc[0:64, 2 * j:2 * j + 1]))
                c.v(nc.vector.tensor_copy(lnpair[64:128, j:j + 1],
                                          lngam_bc[64:128, 2 * j + 1:2 * j + 2]))
                c.v(nc.vector.tensor_copy(alpha_pair[0:64, j:j + 1],
                                          alpha_bc[0:64, 2 * j:2 * j + 1]))
                vv = c.v(nc.vector.tensor_copy(alpha_pair[64:128, j:j + 1],
                                               alpha_bc[64:128, 2 * j + 1:2 * j + 2]))
            c.wait("s", "v", vv)
            c.wait("s", "g", self.consts_g)
            sv = c.s(nc.scalar.activation(gC_pair[:], lnpair[:], AF.Exp,
                                          bias=0.0, scale=self.consts[:, 2:3]))
            state_psum_free = vv
            gv = c.g(nc.gpsimd.iota(cidx[:], pattern=[[1, 128]], base=0,
                                    channel_multiplier=0,
                                    allow_small_or_imprecise_dtypes=True))
            gv = c.g(nc.gpsimd.iota(widx[:], pattern=[[0, 64]], base=127,
                                    channel_multiplier=-1,
                                    allow_small_or_imprecise_dtypes=True))
            gv = c.g(nc.gpsimd.iota(mdif[:], pattern=[[1, 128]], base=-1,
                                    channel_multiplier=-1,
                                    allow_small_or_imprecise_dtypes=True))
            c.wait("v", "g", gv)
            vv = c.v(nc.vector.tensor_scalar_max(mdif[:], mdif[:], 0.0))
            c.wait("v", "g", self.eye_ready)
            vv_eye = c.v(nc.vector.tensor_copy(eye_bf[:], self.eye128[:]))
            c.wait("s", "v", vv)
            for h in range(16):
                sv = c.s(nc.scalar.activation(
                    gw_tab[:, h * 64:(h + 1) * 64], widx[:], AF.Exp,
                    bias=0.0, scale=lngam_bc[:, h:h + 1]))
            for j in range(8):
                sv = c.s(nc.scalar.activation(
                    gp2_tab[:, j * 128:(j + 1) * 128], cidx[:], AF.Exp,
                    bias=0.0, scale=lnpair[:, j:j + 1]))
            for h in range(16):
                sv = c.s(nc.scalar.activation(
                    MT_tab[:, h * 128:(h + 1) * 128], mdif[:], AF.Exp,
                    bias=0.0, scale=lngam_bc[:, h:h + 1]))
            c.wait("g", "s", sv)
            for h in range(16):
                gv = c.g(nc.gpsimd.affine_select(
                    out=MT_tab[:, h * 128:(h + 1) * 128],
                    in_=MT_tab[:, h * 128:(h + 1) * 128],
                    compare_op=ALU.is_ge, fill=0.0, base=-1,
                    pattern=[[1, 128]], channel_multiplier=-1))
            tab_g = gv
            tab_s = sv
            c.wait("g", "s", tab_s)
            c.wait("g", "v", c.n["v"])
            c.wait("g", "t", c.n["t"])
            tsc.close()

            state = {"psum_free": state_psum_free}
            st_cons = {"slab_t": [0, 0], "slab_v": [0, 0]}
            last_u_v = 0
            last_pair_t = 0
            last_vevac = 0
            last_wkg_v = 0
            last_proj_t = 0
            last_readend_s = 0
            prev_rd_t = 0
            u_zero_g = 0

            def slab_col(slot, dc, col):
                return slot * 8 * SW + dc * SW + col

            LB, LG, LC = P1_LIMIT or (B_LOC, 4, 4)
            for b in range(LB):
                c.wait("v", "t", c.n["t"])
                for k4 in range(4):
                    u_zero_g = c.v(nc.vector.tensor_scalar_mul(
                        U_tab[:, k4 * 128:(k4 + 1) * 128], self.ones128[:], 0.0))
                for gs in range(LG):
                    g = b * 4 + gs
                    slot = g % 2
                    sl = slice(g * 512, (g + 1) * 512)
                    rep_v = self.ln_group(g, p_s, p_q, state)
                    c.wait("v", "t", st_cons["slab_t"][slot])
                    c.wait("v", "v", st_cons["slab_v"][slot])
                    c.wait("v", "s", last_readend_s)
                    for dc in range(8):
                        if gs == 0:
                            c.v(nc.vector.tensor_scalar_mul(
                                slab[:, slab_col(slot, dc, 0):slab_col(slot, dc, 1)],
                                self.rep[:, 0:1], 0.0))
                        else:
                            pslot = (g - 1) % 2
                            c.v(nc.vector.tensor_copy(
                                slab[:, slab_col(slot, dc, 0):slab_col(slot, dc, 1)],
                                slab[:, slab_col(pslot, dc, 512):
                                     slab_col(pslot, dc, 513)]))
                        xn_v = self.xn_chunk(
                            slab[:, slab_col(slot, dc, 1):slab_col(slot, dc, 513)],
                            dc, sl, rep_v)
                    state["rep_free"] = xn_v
                    slab_ready_v = xn_v

                    for cl in range(LC):
                        ch_cols = (1 + cl * 128, 1 + (cl + 1) * 128)
                        if "v" not in P1_PARTS:
                            continue
                        c.wait("t", "v", slab_ready_v)
                        c.wait("t", "wws", q_w)
                        for half in range(2):
                            c.wait("t", "v", last_vevac)
                            for dc in range(8):
                                mm = nc.tensor.matmul(
                                    pV[:],
                                    slab[:, slab_col(slot, dc, ch_cols[0]):
                                         slab_col(slot, dc, ch_cols[1])],
                                    WwT_bf[:, dc * 1024 + half * 512:
                                           dc * 1024 + (half + 1) * 512],
                                    start=(dc == 0), stop=(dc == 7))
                            tv = c.inc(mm, "t")
                            c.wait("v", "t", tv)
                            c.wait("v", "t", last_pair_t)
                            last_vevac = c.v(nc.vector.tensor_copy(
                                vbuf[:, half * 512:(half + 1) * 512], pV[:]))
                        for dc in (range(8) if "wk" in P1_PARTS else ()):
                            ptr = pTrA if dc % 2 == 0 else pTrB
                            c.wait("t", "v", state.get(("ptr", dc % 2), 0))
                            tv = c.inc(nc.tensor.transpose(
                                ptr[:, 0:128],
                                slab[:, slab_col(slot, dc, ch_cols[0] - 1):
                                     slab_col(slot, dc, ch_cols[1] - 1)],
                                eye_bf[:]), "t")
                            c.wait("v", "t", tv)
                            c.wait("v", "t", last_pair_t)
                            vv = c.v(nc.vector.tensor_tensor(
                                wkg[:, dc * 128:(dc + 1) * 128],
                                ptr[:, 0:128],
                                gw_tab[:, dc * 128:(dc + 1) * 128],
                                ALU.mult))
                            state[("ptr", dc % 2)] = vv
                        wkg_v = c.n["v"]
                        vbuf_v = last_vevac

                        for j in (range(8) if "pairs" in P1_PARTS else ()):
                            h1, h2 = 2 * j, 2 * j + 1
                            c.wait("t", "v", slab_ready_v)
                            c.wait("t", "v", state.get("smt_done", 0))
                            c.wait("t", "v", last_u_v)
                            c.wait("t", "s", last_readend_s)
                            c.wait("t", "s", tab_s)
                            c.wait("t", "g", tab_g)
                            rkc = (slab_col(slot, j, ch_cols[0]),
                                   slab_col(slot, j, ch_cols[1]))
                            wkc = (rkc[0] - 1, rkc[1] - 1)
                            c.inc(nc.tensor.matmul(
                                pSt[:, 0:128],
                                slab[0:64, wkc[0]:wkc[1]],
                                slab[0:64, rkc[0]:rkc[1]],
                                start=True, stop=True), "t")
                            tv = c.inc(nc.tensor.matmul(
                                pRd[:, 256:384],
                                slab[64:128, wkc[0]:wkc[1]],
                                slab[64:128, rkc[0]:rkc[1]],
                                start=True, stop=True, tile_position=(64, 0)), "t")
                            if "smt" not in P1_PARTS:
                                state["smt_done"] = c.n["v"]
                                continue
                            c.wait("v", "t", tv)
                            c.wait("v", "t", prev_rd_t)
                            smt_v = c.v(nc.vector.tensor_tensor(
                                smt_sb[:, 0:128], pSt[:, 0:128],
                                MT_tab[:, h1 * 128:(h1 + 1) * 128], ALU.mult))
                            smt_v = c.v(nc.vector.tensor_tensor(
                                smt_sb[:, 128:256], pRd[:, 256:384],
                                MT_tab[:, h2 * 128:(h2 + 1) * 128], ALU.mult))
                            rkg_v = c.v(nc.vector.tensor_tensor(
                                rkg_sb[:], slab[:, rkc[0]:rkc[1]],
                                gp2_tab[:, j * 128:(j + 1) * 128], ALU.mult))
                            state["smt_done"] = smt_v
                            c.wait("t", "v", rkg_v)
                            c.wait("t", "v", last_u_v)
                            c.wait("t", "v", vbuf_v)
                            c.wait("t", "s", last_readend_s)
                            if "rd" not in P1_PARTS:
                                continue
                            nc.tensor.matmul(pRd[0:64, 0:128],
                                             vbuf[:, h1 * 64:(h1 + 1) * 64],
                                             smt_sb[:, 0:128],
                                             start=True, stop=False)
                            nc.tensor.matmul(pRd[0:64, 0:128],
                                             U_tab[0:64, j * 64:(j + 1) * 64],
                                             rkg_sb[0:64, :],
                                             start=False, stop=True)
                            nc.tensor.matmul(pRd[64:128, 0:128],
                                             vbuf[:, h2 * 64:(h2 + 1) * 64],
                                             smt_sb[:, 128:256],
                                             start=True, stop=False,
                                             tile_position=(0, 64))
                            tv = c.inc(nc.tensor.matmul(
                                pRd[64:128, 0:128],
                                U_tab[64:128, j * 64:(j + 1) * 64],
                                rkg_sb[64:128, :],
                                start=False, stop=True,
                                tile_position=(64, 64)), "t")
                            prev_rd_t = tv
                            c.wait("s", "t", tv)
                            last_readend_s = c.s(nc.scalar.mul(
                                reads[:, j * 512 + cl * 128:
                                      j * 512 + (cl + 1) * 128],
                                pRd[:, 0:128], alpha_pair[:, j:j + 1]))
                            if "u" not in P1_PARTS:
                                continue
                            c.wait("t", "v", wkg_v)
                            nc.tensor.matmul(pU[0:64, 256:320],
                                             wkg[:, j * 128:j * 128 + 64],
                                             vbuf[:, h1 * 64:(h1 + 1) * 64],
                                             start=True, stop=True)
                            tv = c.inc(nc.tensor.matmul(
                                pU[64:128, 256:320],
                                wkg[:, j * 128 + 64:(j + 1) * 128],
                                vbuf[:, h2 * 64:(h2 + 1) * 64],
                                start=True, stop=True,
                                tile_position=(0, 64)), "t")
                            c.wait("v", "t", tv)
                            last_u_v = c.v(nc.vector.scalar_tensor_tensor(
                                out=U_tab[:, j * 64:(j + 1) * 64],
                                in0=U_tab[:, j * 64:(j + 1) * 64],
                                scalar=gC_pair[:, j:j + 1],
                                in1=pU[:, 256:320], op0=ALU.mult, op1=ALU.add))
                        last_pair_t = c.n["t"]

                    c.wait("t", "s", last_readend_s)
                    proj_evac = state.get("proj_evac", 0)
                    for ohalf in (range(2) if "proj" in P1_PARTS else ()):
                        for jc in range(8):
                            c.dma("qg", "wrs", wrt_s[:, jc * 512:(jc + 1) * 512],
                                  dram["WrT"][jc * 128:(jc + 1) * 128,
                                              ohalf * 512:(ohalf + 1) * 512],
                                  deps=[("t", state.get("wrs_free", 0))])
                        c.wait("t", "wrs", c.n["wrs"])
                        for oc in range(ohalf * 4, ohalf * 4 + 4):
                            oo = oc % 4
                            c.wait("t", "v", proj_evac)
                            for jc in range(8):
                                mm = nc.tensor.matmul(
                                    pPj[:],
                                    wrt_s[:, jc * 512 + oo * 128:
                                          jc * 512 + (oo + 1) * 128],
                                    reads[:, jc * 512:(jc + 1) * 512],
                                    start=(jc == 0), stop=(jc == 7))
                                tv = c.inc(mm, "t")
                            if oc % 4 == 3:
                                state["wrs_free"] = tv
                            c.wait("v", "t", tv)
                            c.v(nc.vector.tensor_add(
                                self.xn_tmp[:], pPj[:],
                                slab[:, slab_col(slot, oc, 1):slab_col(slot, oc, 513)]))
                            proj_evac = c.v(nc.vector.tensor_add(
                                self.xch(oc, sl), self.xch(oc, sl), self.xn_tmp[:]))
                    state["proj_evac"] = proj_evac
                    state["psum_free"] = c.n["v"]
                    st_cons["slab_t"][slot] = c.n["t"]
                    st_cons["slab_v"][slot] = c.n["v"]
            self.x_dep = {"v": c.n["v"]}

    def p2_ffn(self):
        c, nc = self.c, self.nc
        with ExitStack() as sc:
            xn = c.sbuf("xn2", [128, 8 * 512], BF16, scope=sc)
            bias_sb = c.sbuf("fbias", [128, 8], F32, scope=sc)
            p_s = c.psum("p_s", [128, 512], scope=sc)
            p_q = c.psum("p_q", [128, 512], scope=sc)
            py = [c.psum(f"py{i}", [128, 512], scope=sc) for i in range(2)]
            qb = c.dma("qs", "p2b", bias_sb[:, :], self.dram["fbias2d"][:, :])
            c.wait("v", "p2b", qb)
            state = {}
            slot_evac = [0, 0]
            prev_group_mm = 0
            for g in range(8):
                sl = slice(g * 512, (g + 1) * 512)
                rep_v = self.ln_group(g, p_s, p_q, state)
                c.wait("v", "t", prev_group_mm)
                xn_v = 0
                for dc in range(8):
                    xn_v = self.xn_chunk(xn[:, dc * 512:(dc + 1) * 512], dc, sl, rep_v)
                state["rep_free"] = xn_v
                c.wait("t", "v", xn_v)
                c.wait("t", "v", self.M_ready)
                for oc in range(8):
                    slot = oc % 2
                    pb = py[slot]
                    c.wait("t", "v", slot_evac[slot])
                    for jc in range(8):
                        mm = nc.tensor.matmul(
                            pb[:],
                            self.M_sb[:, jc * D + oc * 128: jc * D + (oc + 1) * 128],
                            xn[:, jc * 512:(jc + 1) * 512],
                            start=(jc == 0), stop=(jc == 7))
                    tv = c.inc(mm, "t")
                    c.wait("v", "t", tv)
                    slot_evac[slot] = c.v(nc.vector.scalar_tensor_tensor(
                        out=self.xch(oc, sl), in0=pb[:],
                        scalar=bias_sb[:, oc:oc + 1],
                        in1=self.xch(oc, sl), op0=ALU.add, op1=ALU.add))
                prev_group_mm = c.n["t"]
            self.x_dep = {"v": c.n["v"]}
        c.wait("q", "t", c.n["t"])
        self.m_scope.close()

    def p3_attn(self):
        c, nc = self.c, self.nc
        dram = self.dram
        xs1, xs2 = self.scr["xs1"], self.scr["xs2"]
        xnscr, cumpad = self.scr["xnscr"], self.scr["cumpad"]

        with ExitStack() as sc:
            xn_sp = c.sbuf("xn_sp", [128, 8 * 512], BF16, scope=sc)
            p_s = c.psum("p_s3", [128, 512], scope=sc)
            p_q = c.psum("p_q3", [128, 512], scope=sc)
            state = {}
            st_q = 0
            for g in range(8):
                sl = slice(g * 512, (g + 1) * 512)
                rep_v = self.ln_group(g, p_s, p_q, state)
                c.wait("v", "xnw", st_q)
                xn_v = 0
                for dc in range(8):
                    xn_v = self.xn_chunk(xn_sp[:, dc * 512:(dc + 1) * 512],
                                         dc, sl, rep_v)
                state["rep_free"] = xn_v
                c.wait("q", "v", xn_v)
                for dc in range(8):
                    c.dma("qs", "xnw", xnscr[dc * 128:(dc + 1) * 128, sl],
                          xn_sp[:, dc * 512:(dc + 1) * 512])
                    st_q = c.n["xnw"]
            self.wait_x("q")
            for dc in range(8):
                c.dma("qs", "xsp", xs1[dc * 128:(dc + 1) * 128, :], self.xch(dc))
            c.wait("q", "xsp", c.n["xsp"])
            c.wait("q", "xnw", c.n["xnw"])
        c.barrier()
        self.x_scope.close()

        if P3_DEBUG == "cum":
            self.p3_cum_build()
            with ExitStack() as sc:
                tb = c.sbuf("dbgc", [16, 4096], F32, scope=sc)
                q1 = c.dma("qs", "dbg", tb[:, :], self.scr["cumpad"][:, :],
                           deps=[("cpw", c.n.get("cpw", 0))])
                c.wait("q", "dbg", q1)
                c.dma("qs", "so", self.out[0:16, 0:4096], tb[:])
            c.wait("q", "so", c.n["so"])
            self.x_dep = {"so": c.n["so"]}
            return
        if P3_DEBUG == "xn":
            with ExitStack() as sc:
                tb = c.sbuf("dbg", [128, 4096], F32, scope=sc)
                for dc in range(8):
                    q1 = c.dma("qg", "dbg", tb[:, :],
                               self.scr["xnscr"][dc * 128:(dc + 1) * 128, :],
                               deps=[("so", c.n.get("so", 0))])
                    c.wait("q", "dbg", q1)
                    c.dma("qs", "so", self.out[dc * 128:(dc + 1) * 128, :], tb[:])
                    c.wait("g", "so", c.n["so"])
            c.wait("q", "so", c.n["so"])
            self.x_dep = {"so": c.n["so"]}
            return
        self.p3_cum_build()
        c.barrier()
        self.p3_main()

    def p3_cum_build(self):
        c, nc = self.c, self.nc
        dram = self.dram
        cumpad = self.scr["cumpad"]
        with ExitStack() as sc:
            rpet = c.sbuf("rpet", [128, 16 * 16], F32, scope=sc)
            ones5 = c.sbuf("ones5", [128, 512], F32, scope=sc)
            ltri = c.sbuf("ltri", [128, 512], F32, scope=sc)
            cum_sb = c.sbuf("cum_sb", [16, 2048], F32, scope=sc)
            padt = c.sbuf("padt", [16, 2048], F32, scope=sc)
            p_c = c.psum("p_c", [16, 512], scope=sc)
            q1 = 0
            for uc in range(16):
                q1 = c.dma("qs", "rpe", rpet[:, uc * 16:(uc + 1) * 16],
                           dram["rpeT"][uc * 128:(uc + 1) * 128, :])
            gvp = c.g(nc.gpsimd.memset(padt[:], PAD))
            gvo = c.g(nc.gpsimd.memset(ones5[:], 1.0))
            c.wait("t", "rpe", q1)
            ev = 0
            for mg in range(4):
                for uc in range(16):
                    c.wait("g", "g", gvo)
                    c.wait("g", "t", c.n["t"])
                    gv = c.g(nc.gpsimd.affine_select(
                        out=ltri[:], in_=ones5[:], compare_op=ALU.is_ge,
                        fill=0.0, base=mg * 512 - uc * 128,
                        pattern=[[1, 512]], channel_multiplier=-1))
                    c.wait("t", "g", gv)
                    c.wait("t", "v", ev)
                    tv = c.inc(nc.tensor.matmul(
                        p_c[:], rpet[:, uc * 16:(uc + 1) * 16], ltri[:],
                        start=(uc == 0), stop=(uc == 15)), "t")
                c.wait("v", "t", tv)
                ev = c.v(nc.vector.tensor_copy(
                    cum_sb[:, mg * 512:(mg + 1) * 512], p_c[:]))
            c.wait("q", "g", gvp)
            c.wait("q", "v", ev)
            c.dma("qs", "cpw", cumpad[:, 0:2048], padt[:, :])
            c.dma("qs", "cpw", cumpad[:, 2048:4096], cum_sb[:, :])
            c.wait("q", "cpw", c.n["cpw"])

    def p3_main(self):
        c, nc = self.c, self.nc
        dram = self.dram
        xs1, xs2, xnscr, cumpad = (self.scr["xs1"], self.scr["xs2"],
                                   self.scr["xnscr"], self.scr["cumpad"])
        with ExitStack() as sc:
            qkw_sb = c.sbuf("qkw_sb", [128, 8 * 2048], BF16, scope=sc)
            vw_sb = c.sbuf("vw_sb", [128, 8 * 1024], BF16, scope=sc)
            ow_sb = c.sbuf("ow_sb", [128, 8 * 1024], BF16, scope=sc)
            xn_b = c.sbuf("xn_b", [128, 8 * 2048], BF16, scope=sc)
            QK = c.sbuf("QK", [64, 2 * 2048], BF16, scope=sc)
            Vt = c.sbuf("Vt", [128, 16 * 1024], BF16, scope=sc)
            mix = c.sbuf("mix", [128, 8 * 2048], BF16, scope=sc)
            Hs = c.sbuf("Hs", [128, 2816], BF16, scope=sc)
            Ssb = c.sbuf("Ssb", [128, 2048], F32, scope=sc)
            Psb = c.sbuf("Psb", [128, 2048], BF16, scope=sc)
            PTsb = c.sbuf("PTsb", [128, 16 * 128], BF16, scope=sc)
            diag = c.sbuf("diag3", [128, 128], BF16, scope=sc)
            mrow = c.sbuf("mrow", [128, 4], F32, scope=sc)
            xld = c.sbuf("xld", [128, 512], F32, scope=sc)
            xout = c.sbuf("xout", [128, 512], F32, scope=sc)
            pQK = c.psum("pQK", [128, 512], scope=sc)
            pV2 = c.psum("pV2", [128, 512], scope=sc)
            pS = c.psum("pS", [128, 512], scope=sc)
            pPT = c.psum("pPT", [128, 512], scope=sc)
            pPT2 = c.psum("pPT2", [128, 512], scope=sc)
            pMix = c.psum("pMix", [128, 512], scope=sc)
            pY = c.psum("pY", [128, 512], scope=sc)

            for dc in range(8):
                c.dma("qg", "w3", qkw_sb[:, dc * 2048:(dc + 1) * 2048],
                      dram["qkw"][dc * 128:(dc + 1) * 128, :])
                c.dma("qg", "w3", vw_sb[:, dc * 1024:(dc + 1) * 1024],
                      dram["vw"][dc * 128:(dc + 1) * 128, :])
                c.dma("qg", "w3", ow_sb[:, dc * 1024:(dc + 1) * 1024],
                      dram["ow"][dc * 128:(dc + 1) * 128, :])
            qw3 = c.n["w3"]
            state = {}
            for b in range(B_LOC):
                qx = 0
                for dc in range(8):
                    qx = c.dma("qs", "xn3",
                               xn_b[:, dc * 2048:(dc + 1) * 2048],
                               xnscr[dc * 128:(dc + 1) * 128, b * T:(b + 1) * T],
                               deps=[("t", state.get("xnb_f", 0))])
                c.wait("t", "xn3", qx)
                c.wait("t", "w3", qw3)

                def xg(dc, g):
                    return xn_b[:, dc * 2048 + g * 512: dc * 2048 + (g + 1) * 512]

                for g in range(4):
                    for tt in range(4):
                        ti = g * 4 + tt
                        for half in range(2):
                            c.wait("t", "v", state.get("pv2_f", 0))
                            for dc in range(8):
                                mm = nc.tensor.matmul(
                                    pV2[:],
                                    xg(dc, g)[:, tt * 128:(tt + 1) * 128],
                                    vw_sb[:, dc * 1024 + half * 512:
                                          dc * 1024 + (half + 1) * 512],
                                    start=(dc == 0), stop=(dc == 7))
                            tv = c.inc(mm, "t")
                            c.wait("v", "t", tv)
                            state["pv2_f"] = c.v(nc.vector.tensor_copy(
                                Vt[:, ti * 1024 + half * 512:
                                   ti * 1024 + (half + 1) * 512], pV2[:]))

                for h in range(16):
                    for qk in range(2):
                        for g in range(4):
                            c.wait("t", "v", state.get("pqk_f", 0))
                            for dc in range(8):
                                mm = nc.tensor.matmul(
                                    pQK[0:64, :],
                                    qkw_sb[:, dc * 2048 + h * 128 + qk * 64:
                                           dc * 2048 + h * 128 + qk * 64 + 64],
                                    xg(dc, g), start=(dc == 0), stop=(dc == 7))
                            tv = c.inc(mm, "t")
                            c.wait("v", "t", tv)
                            c.wait("v", "t", state.get("qk_cons", 0))
                            state["pqk_f"] = c.v(nc.vector.tensor_copy(
                                QK[:, qk * 2048 + g * 512:
                                   qk * 2048 + (g + 1) * 512], pQK[0:64, :]))
                    qk_v = c.n["v"]
                    c.wait("t", "v", qk_v)
                    if P3_BARRIER >= 1:
                        c.barrier()
                    qh = c.dma("qg", "hs", Hs[:, :],
                               _AP(tensor=cumpad.tensor, offset=h * 4096 + 1153,
                                   ap=[[1, 128], [1, 2816]]),
                               deps=[("v", state.get("hs_f", 0)),
                                     ("s", state.get("hs_fs", 0))])
                    c.wait("v", "hs", qh)
                    c.wait("s", "hs", qh)
                    for qt in range(16):
                        if P3_BARRIER >= 2:
                            c.barrier()
                        nk = (qt // 4 + 1) * 512
                        for kg in range(nk // 512):
                            c.wait("v", "s", state.get("hs_fs", 0))
                            c.wait("t", "v", state.get("pS_f", 0))
                            tv = c.inc(nc.tensor.matmul(
                                pS[:],
                                QK[0:64, qt * 128:(qt + 1) * 128],
                                QK[0:64, 2048 + kg * 512:2048 + (kg + 1) * 512],
                                start=True, stop=True, tile_position=(0, 0)), "t")
                            m_off = qt - 4 * kg
                            colbase = 128 * (m_off + 3)
                            c.wait("v", "t", tv)
                            hrev = Hs[:, colbase + 511:colbase - 1:-1] \
                                if colbase > 0 else Hs[:, colbase + 511::-1]
                            state["pS_f"] = c.v(nc.vector.scalar_tensor_tensor(
                                out=Ssb[:, kg * 512:(kg + 1) * 512], in0=pS[:],
                                scalar=self.consts[:, 1:2], in1=hrev,
                                op0=ALU.mult, op1=ALU.add))
                        state["hs_f"] = state["pS_f"]
                        if P3_DEBUG == "soft" and b == 0 and h == 0 and qt == 0:
                            state["dump_soft"] = True
                        if P3_DEBUG == "ssb" and b == 0 and h == 0 and qt == 0:
                            c.wait("q", "v", state["pS_f"])
                            c.dma("qs", "so", self.out[0:128, 0:512],
                                  Ssb[:, 0:512])
                            hv = 0
                            for hc in range(2):
                                c.wait("v", "so", c.n["so"])
                                hv = c.v(nc.vector.tensor_copy(
                                    Ssb[:, 0:1408], Hs[:, hc * 1408:(hc + 1) * 1408]))
                                c.wait("q", "v", hv)
                                c.dma("qs", "so",
                                      self.out[0:128, 512 + hc * 1408:
                                               512 + (hc + 1) * 1408],
                                      Ssb[:, 0:1408])
                            c.wait("q", "so", c.n["so"])
                            self.x_dep = {"so": c.n["so"]}
                            return
                        c.wait("v", "s", state.get("hs_fs", 0))
                        vv = c.v(nc.vector.reduce_max(
                            mrow[:, 0:1], Ssb[:, 0:nk], axis=mybir.AxisListType.X))
                        c.wait("v", "v", vv)
                        vv = c.v(nc.vector.tensor_scalar_mul(
                            mrow[:, 1:2], mrow[:, 0:1], -1.0))
                        c.wait("s", "v", vv)
                        c.wait("s", "t", state.get("Psb_f", 0))
                        sv = c.s(nc.scalar.activation(
                            Psb[:, 0:nk], Ssb[:, 0:nk], AF.Exp,
                            bias=mrow[:, 1:2], scale=1.0,
                            accum_out=mrow[:, 2:3]))
                        state["hs_fs"] = sv
                        c.wait("v", "s", sv)
                        vv = c.v(nc.vector.reciprocal(mrow[:, 3:4], mrow[:, 2:3]))
                        c.wait("v", "v", vv)
                        c.wait("v", "t", state.get("diag_f", 0))
                        dv = c.v(nc.vector.tensor_scalar_mul(
                            diag[:], self.eye128[:], mrow[:, 3:4]))
                        if state.get("dump_soft"):
                            state["dump_soft"] = False
                            c.wait("v", "s", sv)
                            pv32 = c.v(nc.vector.tensor_copy(Ssb[:, 0:512],
                                                             Psb[:, 0:512]))
                            c.wait("q", "v", pv32)
                            c.dma("qs", "so", self.out[0:128, 0:512], Ssb[:, 0:512])
                            dv0 = c.v(nc.vector.tensor_scalar_mul(
                                diag[:], self.eye128[:], mrow[:, 3:4]))
                            c.wait("v", "so", c.n["so"])
                            dv32 = c.v(nc.vector.tensor_copy(Ssb[:, 0:128], diag[:]))
                            c.wait("q", "v", dv32)
                            c.dma("qs", "so", self.out[0:128, 512:640], Ssb[:, 0:128])
                            c.wait("q", "v", c.n["v"])
                            c.dma("qs", "so", self.out[0:128, 640:644], mrow[:, 0:4])
                            c.wait("q", "so", c.n["so"])
                            self.x_dep = {"so": c.n["so"]}
                            return
                        for kc in range(nk // 128):
                            pt_bank = pPT if kc % 2 == 0 else pPT2
                            c.wait("t", "v", dv)
                            c.wait("t", "s", sv)
                            c.wait("t", "v", state.get(("pt_f", kc % 2), 0))
                            tv = c.inc(nc.tensor.matmul(
                                pt_bank[:, 0:128],
                                Psb[:, kc * 128:(kc + 1) * 128], diag[:],
                                start=True, stop=True), "t")
                            c.wait("v", "t", tv)
                            state[("pt_f", kc % 2)] = c.v(nc.vector.tensor_copy(
                                PTsb[:, kc * 128:(kc + 1) * 128],
                                pt_bank[:, 0:128]))
                        state["diag_f"] = c.n["t"]
                        state["Psb_f"] = c.n["t"]
                        ptv = c.n["v"]
                        c.wait("t", "v", ptv)
                        c.wait("t", "v", state.get("pmix_f", 0))
                        nkc2 = nk // 128
                        for kc in range(nkc2):
                            mm = nc.tensor.matmul(
                                pMix[0:64, 0:128] if h % 2 == 0
                                else pMix[64:128, 0:128],
                                Vt[:, kc * 1024 + (h // 2) * 128 + (h % 2) * 64:
                                   kc * 1024 + (h // 2) * 128 + (h % 2) * 64 + 64],
                                PTsb[:, kc * 128:(kc + 1) * 128],
                                start=(kc == 0), stop=(kc == nkc2 - 1),
                                **({} if h % 2 == 0 else
                                   dict(tile_position=(0, 64))))
                        tv = c.inc(mm, "t")
                        c.wait("v", "t", tv)
                        ro = slice(0, 64) if h % 2 == 0 else slice(64, 128)
                        state["pmix_f"] = c.v(nc.vector.tensor_copy(
                            mix[ro, (h // 2) * 2048 + qt * 128:
                                (h // 2) * 2048 + (qt + 1) * 128],
                            pMix[ro, 0:128]))
                        if (P3_DEBUG == "mix" and b == 0 and h == 1
                                and qt == 0):
                            c.wait("v", "v", state["pmix_f"])
                            dm = c.v(nc.vector.tensor_copy(Ssb[:, 1792:1796],
                                                           mrow[:, 0:4]))
                            dd = c.v(nc.vector.tensor_copy(Ssb[:, 1536:1664],
                                                           diag[:]))
                            c.wait("q", "v", dd)
                            c.dma("qs", "so", self.out[0:128, 1792:1796],
                                  Ssb[:, 1792:1796])
                            c.dma("qs", "so", self.out[0:128, 1796:1924],
                                  Ssb[:, 1536:1664])
                            d1 = c.v(nc.vector.tensor_copy(
                                Ssb[:, 0:512], PTsb[:, 0:512]))
                            c.wait("q", "v", d1)
                            c.dma("qs", "so", self.out[0:128, 0:512],
                                  Ssb[:, 0:512])
                            c.wait("v", "so", c.n["so"])
                            d2 = c.v(nc.vector.tensor_copy(
                                Ssb[:, 0:256], mix[:, 0:256]))
                            c.wait("q", "v", d2)
                            c.dma("qs", "so", self.out[0:128, 512:768],
                                  Ssb[:, 0:256])
                            c.wait("v", "so", c.n["so"])
                            d3 = c.v(nc.vector.tensor_copy(
                                Ssb[:, 0:1024], Vt[:, 0:1024]))
                            c.wait("q", "v", d3)
                            c.dma("qs", "so", self.out[0:128, 768:1792],
                                  Ssb[:, 0:1024])
                            c.wait("q", "so", c.n["so"])
                            self.x_dep = {"so": c.n["so"]}
                            return
                    state["qk_cons"] = c.n["t"]

                c.wait("t", "v", c.n["v"])
                for g in range(4):
                    sl_abs = slice(b * T + g * 512, b * T + (g + 1) * 512)
                    for oc in range(8):
                        xslot = 0
                        qx = c.dma("qs", "xr0", xld[:, 0:512],
                                   xs1[oc * 128:(oc + 1) * 128, sl_abs],
                                   deps=[("v", state.get(("xld", 0), 0))])
                        c.wait("t", "v", state.get("pY_f", 0))
                        for jc in range(8):
                            mm = nc.tensor.matmul(
                                pY[:],
                                ow_sb[:, jc * 1024 + oc * 128:
                                      jc * 1024 + (oc + 1) * 128],
                                mix[:, jc * 2048 + g * 512:
                                    jc * 2048 + (g + 1) * 512],
                                start=(jc == 0), stop=(jc == 7))
                        tv = c.inc(mm, "t")
                        c.wait("v", "t", tv)
                        c.wait("v", "xr0", qx)
                        c.wait("v", "xo0", state.get(("xout", 0), 0))
                        pY_f = c.v(nc.vector.tensor_add(
                            xout[:, 0:512], pY[:], xld[:, 0:512]))
                        state["pY_f"] = pY_f
                        state[("xld", xslot)] = pY_f
                        c.wait("q", "v", pY_f)
                        c.dma("qs", "xo0",
                              xs2[oc * 128:(oc + 1) * 128, sl_abs],
                              xout[:, 0:512])
                        state[("xout", 0)] = c.n["xo0"]
                state["xnb_f"] = c.n["t"]
            c.wait("q", "xo0", c.n["xo0"])
        self.x_dep = {"xo0": c.n["xo0"]}

    def p4_moe(self):
        c, nc = self.c, self.nc
        dram = self.dram
        if "p3" in self.phases:
            xs2 = self.scr["xs2"]
        else:
            xs2 = dram["x"]
            self.wait_x("q")
            c.barrier()
            self.x_scope.close()
        G = 1024
        NG = TOK // G
        with ExitStack() as sc:
            xg = c.sbuf("xg4", [128, 8 * G], F32, scope=sc)
            xn4 = c.sbuf("xn4", [128, 8 * G], BF16, scope=sc)
            hbuf = c.sbuf("hbuf", [128, 32 * G], BF16, scope=sc)
            oacc = c.sbuf("oacc", [128, 8 * G], F32, scope=sc)
            w1b = c.sbuf("w1b", [128, 2 * 8 * 128], BF16, scope=sc)
            w2b = c.sbuf("w2b", [128, 2 * 32 * 128], BF16, scope=sc)
            cwr = c.sbuf("cwr", [128, G], BF16, scope=sc)
            cw9 = c.sbuf("cw9", [9, G], BF16, scope=sc)
            wg_sb = c.sbuf("wg_sb", [128, 8 * 8], BF16, scope=sc)
            b2s_sb = c.sbuf("b2s_sb", [9, D], BF16, scope=sc)
            b1_sb = c.sbuf("b1_sb", [128, NE * 32], F32, scope=sc)
            bs1_sb = c.sbuf("bs1_sb", [128, 32], F32, scope=sc)
            sel8 = c.sbuf("sel8", [8, 8 * 128], BF16, scope=sc)
            grow4 = c.sbuf("grow4", [128, 16], F32, scope=sc)
            xout4 = c.sbuf("xout4", [128, 512], F32, scope=sc)
            p_s = c.psum("p_s4", [128, 512], scope=sc)
            p_q = c.psum("p_q4", [128, 512], scope=sc)
            pH1 = c.psum("pH1", [128, 512], scope=sc)
            pH2 = c.psum("pH2", [128, 512], scope=sc)
            pO1 = c.psum("pO1", [128, 512], scope=sc)
            pO2 = c.psum("pO2", [128, 512], scope=sc)
            pG = c.psum("pG", [128, 512], scope=sc)

            for dc in range(8):
                c.dma("qg", "w4s", wg_sb[:, dc * 8:(dc + 1) * 8],
                      dram["Wg"][dc * 128:(dc + 1) * 128, :])
            c.dma("qg", "w4s", b2s_sb[:, :], dram["b2s"][:, :])
            c.dma("qs", "w4f", b1_sb[:, :], dram["b1_2d"][:, :])
            c.dma("qs", "w4f", bs1_sb[:, :], dram["bs1_2d"][:, :])
            qconst = c.dma("qg", "w4s", cw9[8:9, :],
                           dram["c01"][0:1, 0:G])
            gv = c.g(nc.gpsimd.memset(sel8[:], 1.0))
            c.wait("g", "g", gv)
            selg = 0
            for e in range(NE):
                selg = c.g(nc.gpsimd.affine_select(
                    out=sel8[:, e * 128:(e + 1) * 128],
                    in_=sel8[:, e * 128:(e + 1) * 128],
                    compare_op=ALU.is_equal, fill=0.0,
                    base=-e, pattern=[[0, 128]], channel_multiplier=1))
            state = {}
            for g4 in range(NG):
                gsl = slice(g4 * G, (g4 + 1) * G)
                qx = 0
                for dc in range(8):
                    qx = c.dma("qs", "x4", xg[:, dc * G:(dc + 1) * G],
                               xs2[dc * 128:(dc + 1) * 128, gsl],
                               deps=[("v", state.get("xg_f", 0)),
                                     ("s", state.get("xg_fs", 0)),
                                     ("xo0", self.x_dep.get("xo0", 0))])
                c.wait("t", "x4", qx)
                c.wait("s", "x4", qx)
                c.wait("v", "x4", qx)
                xn_v = 0
                for sub in range(2):
                    xsrc = lambda dc, s=sub: xg[:, dc * G + s * 512:
                                                dc * G + (s + 1) * 512]
                    rep_v = self.ln_group(g4 * 2 + sub, p_s, p_q, state, xsrc=xsrc)
                    c.wait("v", "t", state.get("xn_cons", 0))
                    for dc in range(8):
                        xn_v = self.xn_chunk(
                            xn4[:, dc * G + sub * 512: dc * G + (sub + 1) * 512],
                            dc, slice(0, 512), rep_v, xsrc=xsrc)
                    state["rep_free"] = xn_v
                c.wait("t", "v", xn_v)
                c.wait("t", "w4s", c.n["w4s"])
                c.wait("t", "g", selg)
                _ss_save = c.self_sync
                c.self_sync = True
                for tt in range(8):
                    c.wait("t", "v", state.get("pg_f", 0))
                    c.wait("t", "s", state.get("pg_fs", 0))
                    for dc in range(8):
                        mm = nc.tensor.matmul(
                            pG[0:128, 0:8],
                            xn4[:, dc * G + tt * 128: dc * G + (tt + 1) * 128],
                            wg_sb[:, dc * 8:(dc + 1) * 8],
                            start=(dc == 0), stop=(dc == 7))
                    tv = c.inc(mm, "t")
                    c.wait("v", "t", tv)
                    vv = c.v(nc.vector.reduce_max(grow4[:, 0:1], pG[:, 0:8],
                                                  axis=mybir.AxisListType.X))
                    c.wait("v", "v", vv)
                    vv = c.v(nc.vector.tensor_scalar_mul(grow4[:, 1:2],
                                                         grow4[:, 0:1], -1.0))
                    c.wait("s", "v", vv)
                    sv = c.s(nc.scalar.activation(
                        grow4[:, 8:16], pG[:, 0:8], AF.Exp,
                        bias=grow4[:, 1:2], scale=1.0,
                        accum_out=grow4[:, 2:3]))
                    state["pg_fs"] = sv
                    qrow = grow4[:, 8:16]
                    c.wait("v", "s", sv)
                    vv = c.v(nc.vector.reduce_max(grow4[:, 3:4], qrow,
                                                  axis=mybir.AxisListType.X))
                    c.wait("v", "v", vv)
                    c.v(nc.vector.tensor_scalar(
                        out=grow4[:, 16:24] if False else self.ln_tmp[:, 0:8],
                        in0=qrow, scalar1=grow4[:, 3:4], scalar2=None,
                        op0=ALU.is_equal))
                    c.v(nc.vector.tensor_scalar(
                        out=self.ln_tmp[:, 8:16], in0=self.ln_tmp[:, 0:8],
                        scalar1=-1.0, scalar2=1.0, op0=ALU.mult, op1=ALU.add))
                    vv = c.v(nc.vector.tensor_mul(self.ln_tmp[:, 16:24], qrow,
                                                  self.ln_tmp[:, 8:16]))
                    c.wait("v", "v", vv)
                    vv = c.v(nc.vector.reduce_max(grow4[:, 4:5],
                                                  self.ln_tmp[:, 16:24],
                                                  axis=mybir.AxisListType.X))
                    c.wait("v", "v", vv)
                    c.v(nc.vector.tensor_add(grow4[:, 5:6], grow4[:, 3:4],
                                             grow4[:, 4:5]))
                    vv = c.v(nc.vector.scalar_tensor_tensor(
                        out=grow4[:, 5:6], in0=grow4[:, 2:3], scalar=1e-9,
                        in1=grow4[:, 5:6], op0=ALU.mult, op1=ALU.add))
                    c.wait("v", "v", vv)
                    vv = c.v(nc.vector.reciprocal(grow4[:, 6:7], grow4[:, 5:6]))
                    c.v(nc.vector.tensor_scalar(
                        out=self.ln_tmp[:, 0:8], in0=qrow,
                        scalar1=grow4[:, 4:5], scalar2=None, op0=ALU.is_ge))
                    vv2 = c.v(nc.vector.tensor_mul(self.ln_tmp[:, 8:16], qrow,
                                                   self.ln_tmp[:, 0:8]))
                    c.wait("v", "v", vv)
                    vv = c.v(nc.vector.tensor_scalar_mul(
                        self.ln_tmp[:, 0:8], self.ln_tmp[:, 8:16],
                        grow4[:, 6:7]))
                    c.wait("t", "v", vv)
                    c.wait("t", "s", state["pg_fs"])
                    c.wait("t", "g", self.eye_ready)
                    tv = c.inc(nc.tensor.transpose(
                        pG[0:8, 128:256], self.ln_tmp[:, 0:8],
                        self.eye128[:]), "t")
                    c.wait("v", "t", tv)
                    c.wait("v", "w4s", qconst)
                    state["pg_f"] = c.v(nc.vector.tensor_copy(
                        cw9[0:8, tt * 128:(tt + 1) * 128], pG[0:8, 128:256]))
                c.self_sync = _ss_save
                gate_v = c.n["v"]
                if P3_DEBUG == "cw" and g4 == 0:
                    c.wait("g", "v", gate_v)
                    c.dma("qg", "dbgw", self.out[0:9, 0:G], cw9[:, :])
                    c.wait("q", "v", gate_v)
                    c.dma("qs", "dbgw2", self.out[16:144, 1024:1040],
                          grow4[:, 0:16])
                    c.wait("q", "dbgw", c.n["dbgw"])
                    c.wait("q", "dbgw2", c.n["dbgw2"])
                    self.x_dep = {"dbgw": c.n["dbgw"]}
                    return

                first_acc = True
                for e in range(NE + 1):
                    shared = (e == NE)
                    self.p4_expert(g4, e, shared, xn4, hbuf, oacc, w1b, w2b,
                                   cwr, cw9, sel8, b1_sb, bs1_sb, b2s_sb,
                                   pH1, pH2, pO1, pO2, pG, state, first_acc)
                    first_acc = False

                for dc in range(8):
                    for half in range(2):
                        c.wait("v", "v", state.get("oacc_done", 0))
                        c.wait("v", "o4", state.get("xout4_f", 0))
                        ov = c.v(nc.vector.tensor_add(
                            xout4[:, 0:512],
                            oacc[:, dc * G + half * 512: dc * G + (half + 1) * 512],
                            xg[:, dc * G + half * 512: dc * G + (half + 1) * 512]))
                        c.wait("q", "v", ov)
                        c.dma("qs", "o4",
                              self.out[dc * 128:(dc + 1) * 128,
                                       g4 * G + half * 512: g4 * G + (half + 1) * 512],
                              xout4[:, 0:512])
                        state["xout4_f"] = c.n["o4"]
                state["xg_f"] = c.n["v"]
                state["xg_fs"] = c.n["s"]
                state["xn_cons"] = c.n["t"]
                state["stat_v"] = c.n["v"]
            c.wait("q", "o4", c.n["o4"])

    def p4_expert(self, g4, e, shared, xn4, hbuf, oacc, w1b, w2b, cwr, cw9,
                  sel8, b1_sb, bs1_sb, b2s_sb, pH1, pH2, pO1, pO2, pG, state,
                  first_acc):
        c, nc = self.c, self.nc
        dram = self.dram
        G = 1024
        if not shared:
            for half in range(2):
                c.wait("t", "v", state.get("cwr_f", 0))
                c.wait("t", "v", state.get("pg_f", 0))
                tv = c.inc(nc.tensor.matmul(
                    pG[:, 0:512],
                    sel8[:, e * 128:(e + 1) * 128],
                    cw9[0:8, half * 512:(half + 1) * 512],
                    start=True, stop=True), "t")
                c.wait("v", "t", tv)
                state["cwr_f"] = c.v(nc.vector.tensor_copy(
                    cwr[:, half * 512:(half + 1) * 512], pG[:, 0:512]))
        cw_v = state.get("cwr_f", 0)
        wsrc1 = dram["Ws1"] if shared else None
        for fc in range(32):
            slot = fc % 2
            if shared:
                q1 = c.dma("qg", f"w1{slot}",
                           w1b[:, slot * 1024:(slot + 1) * 1024]
                           .rearrange("p (a f) -> p a f", a=8),
                           _AP(tensor=dram["Ws1"].tensor, offset=fc * 128,
                               ap=[[4096, 128], [128 * 4096, 8], [1, 128]]),
                           deps=[("t", state.get(("w1f", slot), 0))])
            else:
                q1 = c.dma("qg", f"w1{slot}",
                           w1b[:, slot * 1024:(slot + 1) * 1024]
                           .rearrange("p (a f) -> p a f", a=8),
                           _AP(tensor=dram["W1"].tensor,
                               offset=e * 1024 * 4096 + fc * 128,
                               ap=[[4096, 128], [128 * 4096, 8], [1, 128]]),
                           deps=[("t", state.get(("w1f", slot), 0))])
            c.wait("t", f"w1{slot}", q1)
            for half, pH in ((0, pH1), (1, pH2)):
                c.wait("t", "s", state.get(("ph_f", half), 0))
                for dc in range(8):
                    mm = nc.tensor.matmul(
                        pH[:],
                        w1b[:, slot * 1024 + dc * 128: slot * 1024 + (dc + 1) * 128],
                        xn4[:, dc * G + half * 512: dc * G + (half + 1) * 512],
                        start=(dc == 0), stop=(dc == 7))
                tv = c.inc(mm, "t")
                state[("w1f", slot)] = tv
                c.wait("s", "t", tv)
                c.wait("s", "t", state.get(("hb_cons", fc, half), 0))
                c.wait("s", "w4f", 32)
                bias_ap = (bs1_sb[:, fc:fc + 1] if shared
                           else b1_sb[:, e * 32 + fc: e * 32 + fc + 1])
                sv = c.s(nc.scalar.activation(
                    hbuf[:, fc * G + half * 512: fc * G + (half + 1) * 512],
                    pH[:], AF.Silu, bias=bias_ap, scale=1.0))
                state[("ph_f", half)] = sv
                c.wait("v", "s", sv)
                c.wait("v", "v", cw_v)
                if shared:
                    vv = c.v(nc.vector.tensor_scalar_mul(
                        hbuf[:, fc * G + half * 512: fc * G + (half + 1) * 512],
                        hbuf[:, fc * G + half * 512: fc * G + (half + 1) * 512],
                        SHARED_SCALE))
                else:
                    vv = c.v(nc.vector.tensor_mul(
                        hbuf[:, fc * G + half * 512: fc * G + (half + 1) * 512],
                        hbuf[:, fc * G + half * 512: fc * G + (half + 1) * 512],
                        cwr[:, half * 512:(half + 1) * 512]))
                state[("hb", fc, half)] = vv
        for dc in range(8):
            slot = dc % 2
            if shared:
                q2 = c.dma("qg", f"w2{slot}",
                           w2b[:, slot * 4096:(slot + 1) * 4096]
                           .rearrange("p (a f) -> p a f", a=32),
                           _AP(tensor=dram["Ws2"].tensor, offset=dc * 128,
                               ap=[[1024, 128], [128 * 1024, 32], [1, 128]]),
                           deps=[("t", state.get(("w2f", slot), 0))])
            else:
                q2 = c.dma("qg", f"w2{slot}",
                           w2b[:, slot * 4096:(slot + 1) * 4096]
                           .rearrange("p (a f) -> p a f", a=32),
                           _AP(tensor=dram["W2"].tensor,
                               offset=e * 4096 * 1024 + dc * 128,
                               ap=[[1024, 128], [128 * 1024, 32], [1, 128]]),
                           deps=[("t", state.get(("w2f", slot), 0))])
            c.wait("t", f"w2{slot}", q2)
            for half, pO in ((0, pO1), (1, pO2)):
                c.wait("t", "v", state.get(("po_f", half), 0))
                for fc in range(32):
                    c.wait("t", "v", state.get(("hb", fc, half), 0))
                    mm = nc.tensor.matmul(
                        pO[:],
                        w2b[:, slot * 4096 + fc * 128: slot * 4096 + (fc + 1) * 128],
                        hbuf[:, fc * G + half * 512: fc * G + (half + 1) * 512],
                        start=(fc == 0), stop=(fc == 31 and not shared))
                if shared:
                    mm = nc.tensor.matmul(
                        pO[:], b2s_sb[:, dc * 128:(dc + 1) * 128],
                        cw9[:, half * 512:(half + 1) * 512],
                        start=False, stop=True)
                tv = c.inc(mm, "t")
                state[("w2f", slot)] = tv
                c.wait("v", "t", tv)
                dst = oacc[:, dc * G + half * 512: dc * G + (half + 1) * 512]
                if first_acc:
                    vv = c.v(nc.vector.tensor_copy(dst, pO[:]))
                else:
                    vv = c.v(nc.vector.tensor_add(dst, dst, pO[:]))
                state[("po_f", half)] = vv
                if shared:
                    state["oacc_done"] = vv
        for fc in range(32):
            for half in range(2):
                state[("hb_cons", fc, half)] = c.n["t"]


def make_dft_tables():
    jk = np.outer(np.arange(D, dtype=np.int64), np.arange(D, dtype=np.int64)) % D
    th = 2.0 * np.pi * jk.astype(np.float64) / D
    return np.cos(th).astype(np.float32), np.sin(th).astype(np.float32)


def stage_inputs(inputs):
    Ctab, Stab = make_dft_tables()
    x = inputs["L0_attn_out"]
    qkv = inputs["L1_attn_qkv"]
    qk_cols = []
    for h in range(NH):
        qk_cols.append(qkv[:, 0, h, :])
        qk_cols.append(qkv[:, 1, h, :])
    qkw = np.ascontiguousarray(np.concatenate(qk_cols, axis=1))
    vw = np.ascontiguousarray(
        np.concatenate([qkv[:, 2, h, :] for h in range(NH)], axis=1))
    shared = dict(
        WwT=np.ascontiguousarray(inputs["L0_attn_W_write"].T),
        WrT=np.ascontiguousarray(inputs["L0_attn_W_read"].T),
        decay=inputs["L0_attn_decay"],
        log_alpha=inputs["L0_attn_log_alpha"],
        Ctab=Ctab, Stab=Stab,
        wfr=inputs["L0_ffn_w_freq_real"],
        wfi=inputs["L0_ffn_w_freq_imag"],
        fbias2d=np.ascontiguousarray(inputs["L0_ffn_bias"].reshape(8, 128).T),
        qkw=qkw, vw=vw,
        ow=np.ascontiguousarray(inputs["L1_attn_out_w"].reshape(NH * HD, D)),
        rpeT=np.ascontiguousarray(inputs["L1_attn_rpe_bias"].T),
        Wg=inputs["L1_ffn_Wg"],
        W1=inputs["L1_ffn_W1"],
        b1_2d=np.ascontiguousarray(
            inputs["L1_ffn_b1"].reshape(NE, 32, 128).transpose(2, 0, 1)
            .reshape(128, NE * 32)),
        W2=inputs["L1_ffn_W2"],
        Ws1=inputs["L1_ffn_Ws1"],
        bs1_2d=np.ascontiguousarray(inputs["L1_ffn_bs1"].reshape(32, 128).T),
        Ws2=inputs["L1_ffn_Ws2"],
        b2s=np.ascontiguousarray(np.concatenate(
            [inputs["L1_ffn_b2"], inputs["L1_ffn_bs2"][None, :]], axis=0)),
        c01=np.full((1, 1024), 0.1, np.float32),
    )
    maps = []
    for ci in range(NCORES):
        xc = x[ci * B_LOC:(ci + 1) * B_LOC].reshape(TOK, D)
        m = dict(shared)
        m["x"] = np.ascontiguousarray(xc.T)
        maps.append(m)
    return maps


def kernel(**inputs):
    maps = stage_inputs(inputs)
    nc = build_program()
    res = run_bass_kernel_spmd(nc, maps, core_ids=list(range(NCORES)))
    outs = []
    for ci in range(NCORES):
        o = np.asarray(res.results[ci]["out"])
        outs.append(np.ascontiguousarray(o.T).reshape(B_LOC, T, D))
    return np.concatenate(outs, axis=0).astype(np.float32)


# version 02 (final)
# speedup vs baseline: 1.0079x; 1.0079x over previous
import numpy as np
import ml_dtypes
import concourse.bass as bass
import concourse.mybir as mybir
from concourse.ap import AP as _AP
from concourse.bass_utils import run_bass_kernel_spmd
from contextlib import ExitStack

F32 = mybir.dt.float32
F32R = mybir.dt.float32r
BF16 = mybir.dt.bfloat16
AF = mybir.ActivationFunctionType
ALU = mybir.AluOpType

NCORES = 8
B_LOC = 2
T = 2048
D = 1024
TOK = B_LOC * T
NH = 16
HD = 64
CH = 64
NCH = T // CH
FF = 4096
NE = 8
EPS_LN = 1e-5
PAD = -30000.0
SHARED_SCALE = 0.1

ALL_PHASES = ("p1", "p2", "p3", "p4")
P3_BARRIER = 0
P3_DEBUG = None
P1_PARTS = frozenset(("v", "wk", "pairs", "smt", "rd", "u", "proj"))
P1_LIMIT = None


class Ctx:

    def __init__(self, nc, ectx, self_sync=False):
        self.self_sync = self_sync
        self.nc = nc
        self.ectx = ectx
        self.n = {}
        self.sems = {}
        self.last_wait = {}
        self.engs = {"t": nc.tensor, "v": nc.vector, "s": nc.scalar,
                     "g": nc.gpsimd, "q": nc.sync}
        for s in ("t", "v", "s", "g"):
            self.sems[s] = ectx.enter_context(nc.semaphore(s))
            self.n[s] = 0
        self.streams = []

    def inc(self, inst, name, amt=1):
        self.n[name] += amt
        inst.then_inc(self.sems[name], amt)
        if self.self_sync and name in ("t", "v", "s", "g"):
            self.engs[name].wait_ge(self.sems[name], self.n[name])
            self.last_wait[(name, name)] = self.n[name]
        return self.n[name]

    def t(self, inst):
        return self.inc(inst, "t")

    def v(self, inst):
        return self.inc(inst, "v")

    def s(self, inst):
        return self.inc(inst, "s")

    def g(self, inst):
        return self.inc(inst, "g")

    def stream(self, name):
        if name not in self.sems:
            self.sems[name] = self.ectx.enter_context(self.nc.semaphore(name))
            self.n[name] = 0
            self.streams.append(name)
        return name

    def dma(self, q, stream, dst, src, deps=()):
        ename = {"qs": "q", "qg": "g", "qa": "s"}[q]
        for sname, val in deps:
            self.wait(ename, sname, val)
        eng = {"qs": self.nc.sync, "qg": self.nc.gpsimd, "qa": self.nc.scalar}[q]
        inst = eng.dma_start(out=dst, in_=src)
        return self.inc(inst, self.stream(stream), 16)

    def wait(self, ename, sname, val):
        if val <= 0:
            return
        key = (ename, sname)
        if self.last_wait.get(key, -1) >= val:
            return
        self.last_wait[key] = val
        self.engs[ename].wait_ge(self.sems[sname], val)

    def barrier(self):
        for e in ("t", "v", "s", "g", "q"):
            for sname in ("t", "v", "s", "g", *self.streams):
                ename_of = {"t": "t", "v": "v", "s": "s", "g": "g"}.get(sname)
                if ename_of == e:
                    continue
                self.wait(e, sname, self.n[sname])

    def sbuf(self, name, shape, dt=F32, scope=None):
        return (scope or self.ectx).enter_context(
            self.nc.sbuf_tensor(name, shape, dt))

    def psum(self, name, shape, dt=F32, scope=None):
        return (scope or self.ectx).enter_context(
            self.nc.psum_tensor(name, shape, dt))


def build_program(phases=ALL_PHASES, dev=False):
    nc = bass.Bass("TRN2", target_bir_lowering=False,
                   detect_race_conditions=bool(dev))
    dram = {}

    def din(name, shape, dt=F32):
        dram[name] = nc.dram_tensor(name, shape, dt, kind="ExternalInput").ap()

    din("x", [D, TOK])
    din("WwT", [D, D]); din("WrT", [D, D])
    din("decay", [NH]); din("log_alpha", [NH])
    din("Ctab", [D, D]); din("Stab", [D, D])
    din("wfr", [D, D]); din("wfi", [D, D]); din("fbias2d", [128, 8])
    din("qkw", [D, NH * 2 * HD]); din("vw", [D, NH * HD]); din("ow", [NH * HD, D])
    din("rpeT", [T, NH])
    din("Wg", [D, NE])
    din("W1", [NE, D, FF]); din("b1_2d", [128, 32 * NE])
    din("W2", [NE, FF, D])
    din("Ws1", [D, FF]); din("bs1_2d", [128, 32]); din("Ws2", [FF, D])
    din("b2s", [NE + 1, D]); din("c01", [1, 1024])
    out = nc.dram_tensor("out", [D, TOK], F32, kind="ExternalOutput").ap()

    scr = dict(
        scrA=nc.dram_tensor("scrA", [D, D], BF16, kind="Internal").ap(),
        scrB=nc.dram_tensor("scrB", [D, D], BF16, kind="Internal").ap(),
        cumpad=nc.dram_tensor("cumpad", [NH, 4096], F32, kind="Internal").ap(),
        xnscr=nc.dram_tensor("xnscr", [D, TOK], BF16, kind="Internal").ap(),
        xs1=nc.dram_tensor("xs1", [D, TOK], F32, kind="Internal").ap(),
        xs2=nc.dram_tensor("xs2", [D, TOK], F32, kind="Internal").ap(),
    )

    with ExitStack() as ectx:
        c = Ctx(nc, ectx, self_sync=bool(dev))
        Builder(c, dram, out, scr, phases).run()
    return nc


class Builder:
    def __init__(self, c, dram, out, scr, phases):
        self.c = c
        self.nc = c.nc
        self.dram = dram
        self.out = out
        self.scr = scr
        self.phases = phases
        self.x_dep = {}

    def run(self):
        c, nc = self.c, self.nc
        self.ones_row = c.sbuf("ones_row", [1, 128], F32)
        self.ones_col = c.sbuf("ones_col", [128, 1], F32)
        self.eye128 = c.sbuf("eye", [128, 128], F32)
        self.rep = c.sbuf("rep", [128, 2 * 512], F32)
        self.ln_tmp = c.sbuf("ln_tmp", [128, 3 * 512], F32)
        self.xn_tmp = c.sbuf("xn_tmp", [128, 512], F32)

        self.consts = c.sbuf("consts", [128, 4], F32)
        c.g(nc.gpsimd.memset(self.consts[:, 0:1], EPS_LN))
        c.g(nc.gpsimd.memset(self.consts[:, 1:2], 0.125))
        c.g(nc.gpsimd.memset(self.consts[:, 2:3], 128.0))
        c.g(nc.gpsimd.memset(self.ones_row[:], 1.0))
        self.consts_g = c.g(nc.gpsimd.memset(self.ones_col[:], 1.0))
        self.ones128 = c.sbuf("ones128", [128, 128], F32)
        self.x_scope = ExitStack()
        self.x_sb = c.sbuf("x_sb", [128, 8 * TOK], F32, scope=self.x_scope)
        gv = c.g(nc.gpsimd.memset(self.ones128[:], 1.0))
        c.wait("g", "g", gv)
        self.eye_ready = c.g(nc.gpsimd.affine_select(
            out=self.eye128[:], in_=self.ones128[:], compare_op=ALU.is_equal,
            fill=0.0, base=0, pattern=[[-1, 128]], channel_multiplier=1))

        for dc in range(8):
            c.dma("qs", "sx", self.xch(dc),
                  self.dram["x"][dc * 128:(dc + 1) * 128, :])
        self.x_dep = {"sx": c.n["sx"]}

        if "p1" in self.phases:
            self.p1_ttt()
            c.barrier()
        if "p2" in self.phases:
            self.build_M()
            c.barrier()
        if "p2" in self.phases:
            self.p2_ffn()
            c.barrier()
        if "p3" in self.phases:
            self.p3_attn()
            c.barrier()
        if "p4" in self.phases:
            self.p4_moe()
        else:
            self.store_x_to_out()
        self.final_sync()

    def xch(self, dc, sl=None):
        sl = sl or slice(0, TOK)
        return self.x_sb[:, dc * TOK + sl.start: dc * TOK + sl.stop]

    def wait_x(self, ename):
        for sname, val in self.x_dep.items():
            self.c.wait(ename, sname, val)

    def store_x_to_out(self):
        c = self.c
        self.wait_x("q")
        if "p3" in self.phases:
            if P3_DEBUG is None:
                for dc in range(8):
                    c.dma("qs", "so", self.out[dc * 128:(dc + 1) * 128, :],
                          self.scr["xs2"][dc * 128:(dc + 1) * 128, :])
            c.wait("q", "so", c.n["so"])
        elif False:
            pass
        else:
            for dc in range(8):
                c.dma("qs", "so", self.out[dc * 128:(dc + 1) * 128, :],
                      self.xch(dc))

    def final_sync(self):
        c = self.c
        for sname in c.streams:
            c.wait("q", sname, c.n[sname])

    def ln_group(self, g, p_s, p_q, state, xsrc=None):
        c, nc = self.c, self.nc
        sl = slice(g * 512, (g + 1) * 512)
        if xsrc is None:
            xsrc = lambda dc: self.xch(dc, sl)
            self.wait_x("t")
            self.wait_x("s")
        c.wait("t", "g", self.consts_g)
        c.wait("t", "v", state.get("psum_free", 0))
        for dc in range(8):
            nc.tensor.matmul(p_s[:], self.ones128[:], xsrc(dc),
                             start=(dc == 0), stop=(dc == 7))
        sq = self.ln_tmp
        c.wait("s", "v", state.get("stat_v", 0))
        for dc in range(8):
            slot = dc % 2
            c.wait("s", "t", state.get(("sq", slot), 0))
            sv = c.s(nc.scalar.activation(
                sq[:, slot * 512:(slot + 1) * 512], xsrc(dc),
                AF.Square, bias=0.0, scale=1.0))
            c.wait("t", "s", sv)
            tv = c.inc(nc.tensor.matmul(p_q[:], self.ones128[:],
                                        sq[:, slot * 512:(slot + 1) * 512],
                                        start=(dc == 0), stop=(dc == 7)), "t")
            state[("sq", slot)] = tv
        tA = self.ln_tmp[:, 0:512]
        tB = self.ln_tmp[:, 512:1024]
        tC = self.ln_tmp[:, 1024:1536]
        c.wait("v", "t", tv)
        c.wait("v", "v", state.get("rep_free", 0))
        c.v(nc.vector.tensor_scalar_mul(tA, p_s[:], 1.0 / D))
        c.v(nc.vector.tensor_scalar_mul(tB, p_q[:], 1.0 / D))
        state["psum_free"] = c.n["v"]
        c.v(nc.vector.tensor_mul(tC, tA, tA))
        vv = c.v(nc.vector.tensor_sub(tB, tB, tC))
        c.wait("s", "v", vv)
        c.wait("s", "g", self.consts_g)
        sv = c.s(nc.scalar.activation(tC, tB, AF.Sqrt,
                                      bias=self.consts[:, 0:1], scale=1.0))
        c.wait("v", "s", sv)
        c.v(nc.vector.reciprocal(self.rep[:, 0:512], tC))
        vv = c.v(nc.vector.tensor_mul(self.rep[:, 512:1024], tA,
                                      self.rep[:, 0:512]))
        state["stat_v"] = vv
        return vv

    def xn_chunk(self, dst, dc, sl, rep_v, xsrc=None):
        c, nc = self.c, self.nc
        n = sl.stop - sl.start
        x_ap = self.xch(dc, sl) if xsrc is None else xsrc(dc)
        c.wait("v", "v", rep_v)
        c.v(nc.vector.tensor_mul(self.xn_tmp[:, 0:n], x_ap,
                                 self.rep[:, 0:n]))
        return c.v(nc.vector.tensor_sub(
            dst, self.xn_tmp[:, 0:n], self.rep[:, 512:512 + n]))

    def build_M(self):
        c, nc = self.c, self.nc
        dram = self.dram
        scrA, scrB = self.scr["scrA"], self.scr["scrB"]
        self.m_scope = ExitStack()
        self.M_sb = c.sbuf("M_sb", [128, 8 * D], BF16, scope=self.m_scope)
        with ExitStack() as sc:
            cs = c.sbuf("cs_sl", [128, 8 * 512], BF16, scope=sc)
            sst = c.sbuf("ss_sl", [128, 8 * 512], BF16, scope=sc)
            ssn = c.sbuf("ssn_sl", [128, 8 * 512], BF16, scope=sc)
            wr_sl = c.sbuf("wr_sl", [128, 8 * 128], BF16, scope=sc)
            wi_sl = c.sbuf("wi_sl", [128, 8 * 128], BF16, scope=sc)
            ab_sl = c.sbuf("ab_sl", [128, 2 * 8 * 128], BF16, scope=sc)
            evac = c.sbuf("evac_m", [128, 2 * 512], BF16, scope=sc)
            pA = c.psum("pA", [128, 512], scope=sc)
            pB = c.psum("pB", [128, 512], scope=sc)

            def load_cs(col0):
                q = 0
                for lc in range(8):
                    c.dma("qg", "mcs", cs[:, lc * 512:(lc + 1) * 512],
                          dram["Ctab"][lc * 128:(lc + 1) * 128, col0:col0 + 512],
                          deps=[("t", c.n["t"])])
                    q = c.dma("qg", "mcs", sst[:, lc * 512:(lc + 1) * 512],
                              dram["Stab"][lc * 128:(lc + 1) * 128, col0:col0 + 512])
                c.wait("v", "mcs", q)
                vv = 0
                for lc in range(8):
                    vv = c.v(nc.vector.tensor_scalar_mul(
                        ssn[:, lc * 512:(lc + 1) * 512],
                        sst[:, lc * 512:(lc + 1) * 512], -1.0))
                return vv

            for jg in range(2):
                vneg = load_cs(jg * 512)
                for kc in range(8):
                    qw = 0
                    for lc in range(8):
                        c.dma("qg", "mw", wr_sl[:, lc * 128:(lc + 1) * 128],
                              dram["wfr"][lc * 128:(lc + 1) * 128,
                                          kc * 128:(kc + 1) * 128],
                              deps=[("t", c.n["t"])])
                        qw = c.dma("qg", "mw", wi_sl[:, lc * 128:(lc + 1) * 128],
                                   dram["wfi"][lc * 128:(lc + 1) * 128,
                                               kc * 128:(kc + 1) * 128])
                    c.wait("t", "mw", qw)
                    c.wait("t", "v", vneg)
                    for lc in range(8):
                        nc.tensor.matmul(pA[:], wr_sl[:, lc * 128:(lc + 1) * 128],
                                         cs[:, lc * 512:(lc + 1) * 512],
                                         start=(lc == 0), stop=False)
                    for lc in range(8):
                        nc.tensor.matmul(pA[:], wi_sl[:, lc * 128:(lc + 1) * 128],
                                         sst[:, lc * 512:(lc + 1) * 512],
                                         start=False, stop=(lc == 7))
                    for lc in range(8):
                        nc.tensor.matmul(pB[:], wi_sl[:, lc * 128:(lc + 1) * 128],
                                         cs[:, lc * 512:(lc + 1) * 512],
                                         start=(lc == 0), stop=False)
                    for lc in range(8):
                        mm = nc.tensor.matmul(pB[:], wr_sl[:, lc * 128:(lc + 1) * 128],
                                              ssn[:, lc * 512:(lc + 1) * 512],
                                              start=False, stop=(lc == 7))
                    tv = c.inc(mm, "t")
                    c.wait("v", "t", tv)
                    c.v(nc.vector.tensor_copy(evac[:, 0:512], pA[:]))
                    vv = c.v(nc.vector.tensor_copy(evac[:, 512:1024], pB[:]))
                    c.wait("t", "v", vv)
                    c.wait("q", "v", vv)
                    c.dma("qs", "mst", scrA[kc * 128:(kc + 1) * 128,
                                            jg * 512:(jg + 1) * 512], evac[:, 0:512])
                    qa = c.dma("qs", "mst", scrB[kc * 128:(kc + 1) * 128,
                                                 jg * 512:(jg + 1) * 512],
                               evac[:, 512:1024])
                    c.wait("v", "mst", qa)
            self._apass_q = c.n["mst"]
            self.M_ready = 0
            for mg in range(2):
                vneg = load_cs(mg * 512)
                for jc in range(8):
                    qw = 0
                    for kc in range(8):
                        c.dma("qg", "mab", ab_sl[:, kc * 128:(kc + 1) * 128],
                              scrA[kc * 128:(kc + 1) * 128, jc * 128:(jc + 1) * 128],
                              deps=[("t", c.n["t"]), ("mst", self._apass_q)])
                        qw = c.dma("qg", "mab",
                                   ab_sl[:, 1024 + kc * 128:1024 + (kc + 1) * 128],
                                   scrB[kc * 128:(kc + 1) * 128,
                                        jc * 128:(jc + 1) * 128])
                    c.wait("t", "mab", qw)
                    c.wait("t", "v", vneg)
                    for kc in range(8):
                        nc.tensor.matmul(pA[:], ab_sl[:, kc * 128:(kc + 1) * 128],
                                         cs[:, kc * 512:(kc + 1) * 512],
                                         start=(kc == 0), stop=False)
                    for kc in range(8):
                        mm = nc.tensor.matmul(
                            pA[:], ab_sl[:, 1024 + kc * 128:1024 + (kc + 1) * 128],
                            ssn[:, kc * 512:(kc + 1) * 512],
                            start=False, stop=(kc == 7))
                    tv = c.inc(mm, "t")
                    c.wait("v", "t", tv)
                    vv = c.v(nc.vector.tensor_scalar_mul(
                        self.M_sb[:, jc * D + mg * 512: jc * D + (mg + 1) * 512],
                        pA[:], 1.0 / D))
                    c.wait("t", "v", vv)
                    self.M_ready = vv

    def p1_ttt(self):
        c, nc = self.c, self.nc
        dram = self.dram
        CL = 128
        NCHD = T // CL
        SW = 513
        with ExitStack() as sc:
            slab = c.sbuf("slab", [128, 2 * 8 * SW], BF16, scope=sc)
            vbuf = c.sbuf("vbuf", [128, 1024], BF16, scope=sc)
            wkg = c.sbuf("wkg", [128, 1024], BF16, scope=sc)
            reads = c.sbuf("reads", [128, 8 * 512], BF16, scope=sc)
            U_tab = c.sbuf("U_tab", [128, 8 * 64], F32, scope=sc)
            WwT_bf = c.sbuf("WwT_bf", [128, 8 * 1024], BF16, scope=sc)
            wrt_s = c.sbuf("wrt_s", [128, 8 * 512], BF16, scope=sc)
            MT_tab = c.sbuf("MT_tab", [128, 16 * 128], BF16, scope=sc)
            gp2_tab = c.sbuf("gp2_tab", [128, 8 * 128], BF16, scope=sc)
            gw_tab = c.sbuf("gw_tab", [128, 16 * 64], BF16, scope=sc)
            lnpair = c.sbuf("lnpair", [128, 8], F32, scope=sc)
            alpha_pair = c.sbuf("alpha_pair", [128, 8], F32, scope=sc)
            gC_pair = c.sbuf("gC_pair", [128, 8], F32, scope=sc)
            eye_bf = c.sbuf("eye_bf", [128, 128], BF16, scope=sc)
            smt_sb = c.sbuf("smt_sb", [128, 256], BF16, scope=sc)
            rkg_sb = c.sbuf("rkg_sb", [128, 128], F32, scope=sc)
            pSt = c.psum("pSt", [128, 512], scope=sc)
            pU = pSt
            pRd = c.psum("pRd", [128, 512], scope=sc)
            pTrA = c.psum("pTrA", [128, 1024], BF16, scope=sc)
            pTrB = c.psum("pTrB", [128, 1024], BF16, scope=sc)
            pV = c.psum("pV", [128, 512], scope=sc)
            pPj = c.psum("pPj", [128, 512], scope=sc)
            p_s = c.psum("p_s1", [128, 512], scope=sc)
            p_q = c.psum("p_q1", [128, 512], scope=sc)

            for dc in range(8):
                c.dma("qg", "wws", WwT_bf[:, dc * 1024:(dc + 1) * 1024],
                      dram["WwT"][dc * 128:(dc + 1) * 128, :])
            q_w = c.n["wws"]
            tsc = ExitStack()
            lngam_bc = c.sbuf("lngam_bc", [128, 16], F32, scope=tsc)
            alpha_bc = c.sbuf("alpha_bc", [128, 16], F32, scope=tsc)
            grow = c.sbuf("grow", [1, 64], F32, scope=tsc)
            cidx = c.sbuf("cidx", [128, 128], F32, scope=tsc)
            widx = c.sbuf("widx", [128, 64], F32, scope=tsc)
            mdif = c.sbuf("mdif", [128, 128], F32, scope=tsc)
            c.dma("qs", "p1s", grow[0:1, 0:16],
                  dram["decay"].rearrange("(a k) -> a k", a=1))
            q_sm = c.dma("qs", "p1s", grow[0:1, 16:32],
                         dram["log_alpha"].rearrange("(a k) -> a k", a=1))
            c.wait("s", "p1s", q_sm)
            c.s(nc.scalar.activation(grow[0:1, 32:48], grow[0:1, 0:16],
                                     AF.Sigmoid, bias=0.0, scale=1.0))
            sv = c.s(nc.scalar.activation(grow[0:1, 48:64], grow[0:1, 32:48],
                                          AF.Ln, bias=0.0, scale=1.0))
            sv_a = c.s(nc.scalar.activation(grow[0:1, 16:32], grow[0:1, 16:32],
                                            AF.Exp, bias=0.0, scale=1.0))
            c.wait("t", "s", sv_a)
            c.wait("t", "g", self.consts_g)
            tv = c.inc(nc.tensor.matmul(p_s[:, 0:16], self.ones_row[:],
                                        grow[0:1, 48:64], start=True, stop=True), "t")
            tv = c.inc(nc.tensor.matmul(p_q[:, 0:16], self.ones_row[:],
                                        grow[0:1, 16:32], start=True, stop=True), "t")
            c.wait("v", "t", tv)
            c.v(nc.vector.tensor_copy(lngam_bc[:], p_s[:, 0:16]))
            vv = c.v(nc.vector.tensor_copy(alpha_bc[:], p_q[:, 0:16]))
            for j in range(8):
                c.v(nc.vector.tensor_copy(lnpair[0:64, j:j + 1],
                                          lngam_bc[0:64, 2 * j:2 * j + 1]))
                c.v(nc.vector.tensor_copy(lnpair[64:128, j:j + 1],
                                          lngam_bc[64:128, 2 * j + 1:2 * j + 2]))
                c.v(nc.vector.tensor_copy(alpha_pair[0:64, j:j + 1],
                                          alpha_bc[0:64, 2 * j:2 * j + 1]))
                vv = c.v(nc.vector.tensor_copy(alpha_pair[64:128, j:j + 1],
                                               alpha_bc[64:128, 2 * j + 1:2 * j + 2]))
            c.wait("s", "v", vv)
            c.wait("s", "g", self.consts_g)
            sv = c.s(nc.scalar.activation(gC_pair[:], lnpair[:], AF.Exp,
                                          bias=0.0, scale=self.consts[:, 2:3]))
            state_psum_free = vv
            gv = c.g(nc.gpsimd.iota(cidx[:], pattern=[[1, 128]], base=0,
                                    channel_multiplier=0,
                                    allow_small_or_imprecise_dtypes=True))
            gv = c.g(nc.gpsimd.iota(widx[:], pattern=[[0, 64]], base=127,
                                    channel_multiplier=-1,
                                    allow_small_or_imprecise_dtypes=True))
            gv = c.g(nc.gpsimd.iota(mdif[:], pattern=[[1, 128]], base=-1,
                                    channel_multiplier=-1,
                                    allow_small_or_imprecise_dtypes=True))
            c.wait("v", "g", gv)
            vv = c.v(nc.vector.tensor_scalar_max(mdif[:], mdif[:], 0.0))
            c.wait("v", "g", self.eye_ready)
            vv_eye = c.v(nc.vector.tensor_copy(eye_bf[:], self.eye128[:]))
            c.wait("s", "v", vv)
            for h in range(16):
                sv = c.s(nc.scalar.activation(
                    gw_tab[:, h * 64:(h + 1) * 64], widx[:], AF.Exp,
                    bias=0.0, scale=lngam_bc[:, h:h + 1]))
            for j in range(8):
                sv = c.s(nc.scalar.activation(
                    gp2_tab[:, j * 128:(j + 1) * 128], cidx[:], AF.Exp,
                    bias=0.0, scale=lnpair[:, j:j + 1]))
            for h in range(16):
                sv = c.s(nc.scalar.activation(
                    MT_tab[:, h * 128:(h + 1) * 128], mdif[:], AF.Exp,
                    bias=0.0, scale=lngam_bc[:, h:h + 1]))
            c.wait("g", "s", sv)
            for h in range(16):
                gv = c.g(nc.gpsimd.affine_select(
                    out=MT_tab[:, h * 128:(h + 1) * 128],
                    in_=MT_tab[:, h * 128:(h + 1) * 128],
                    compare_op=ALU.is_ge, fill=0.0, base=-1,
                    pattern=[[1, 128]], channel_multiplier=-1))
            tab_g = gv
            tab_s = sv
            c.wait("g", "s", tab_s)
            c.wait("g", "v", c.n["v"])
            c.wait("g", "t", c.n["t"])
            tsc.close()

            state = {"psum_free": state_psum_free}
            st_cons = {"slab_t": [0, 0], "slab_v": [0, 0]}
            last_u_v = 0
            last_pair_t = 0
            last_vevac = 0
            last_wkg_v = 0
            last_proj_t = 0
            last_readend_s = 0
            prev_rd_t = 0
            u_zero_g = 0

            def slab_col(slot, dc, col):
                return slot * 8 * SW + dc * SW + col

            LB, LG, LC = P1_LIMIT or (B_LOC, 4, 4)
            for b in range(LB):
                c.wait("v", "t", c.n["t"])
                for k4 in range(4):
                    u_zero_g = c.v(nc.vector.tensor_scalar_mul(
                        U_tab[:, k4 * 128:(k4 + 1) * 128], self.ones128[:], 0.0))
                for gs in range(LG):
                    g = b * 4 + gs
                    slot = g % 2
                    sl = slice(g * 512, (g + 1) * 512)
                    rep_v = self.ln_group(g, p_s, p_q, state)
                    c.wait("v", "t", st_cons["slab_t"][slot])
                    c.wait("v", "v", st_cons["slab_v"][slot])
                    c.wait("v", "s", last_readend_s)
                    for dc in range(8):
                        if gs == 0:
                            c.v(nc.vector.tensor_scalar_mul(
                                slab[:, slab_col(slot, dc, 0):slab_col(slot, dc, 1)],
                                self.rep[:, 0:1], 0.0))
                        else:
                            pslot = (g - 1) % 2
                            c.v(nc.vector.tensor_copy(
                                slab[:, slab_col(slot, dc, 0):slab_col(slot, dc, 1)],
                                slab[:, slab_col(pslot, dc, 512):
                                     slab_col(pslot, dc, 513)]))
                        xn_v = self.xn_chunk(
                            slab[:, slab_col(slot, dc, 1):slab_col(slot, dc, 513)],
                            dc, sl, rep_v)
                    state["rep_free"] = xn_v
                    slab_ready_v = xn_v

                    for cl in range(LC):
                        ch_cols = (1 + cl * 128, 1 + (cl + 1) * 128)
                        if "v" not in P1_PARTS:
                            continue
                        c.wait("t", "v", slab_ready_v)
                        c.wait("t", "wws", q_w)
                        for half in range(2):
                            c.wait("t", "v", last_vevac)
                            for dc in range(8):
                                mm = nc.tensor.matmul(
                                    pV[:],
                                    slab[:, slab_col(slot, dc, ch_cols[0]):
                                         slab_col(slot, dc, ch_cols[1])],
                                    WwT_bf[:, dc * 1024 + half * 512:
                                           dc * 1024 + (half + 1) * 512],
                                    start=(dc == 0), stop=(dc == 7))
                            tv = c.inc(mm, "t")
                            c.wait("v", "t", tv)
                            c.wait("v", "t", last_pair_t)
                            last_vevac = c.v(nc.vector.tensor_copy(
                                vbuf[:, half * 512:(half + 1) * 512], pV[:]))
                        for dc in (range(8) if "wk" in P1_PARTS else ()):
                            ptr = pTrA if dc % 2 == 0 else pTrB
                            c.wait("t", "v", state.get(("ptr", dc % 2), 0))
                            tv = c.inc(nc.tensor.transpose(
                                ptr[:, 0:128],
                                slab[:, slab_col(slot, dc, ch_cols[0] - 1):
                                     slab_col(slot, dc, ch_cols[1] - 1)],
                                eye_bf[:]), "t")
                            c.wait("v", "t", tv)
                            c.wait("v", "t", last_pair_t)
                            vv = c.v(nc.vector.tensor_tensor(
                                wkg[:, dc * 128:(dc + 1) * 128],
                                ptr[:, 0:128],
                                gw_tab[:, dc * 128:(dc + 1) * 128],
                                ALU.mult))
                            state[("ptr", dc % 2)] = vv
                        wkg_v = c.n["v"]
                        vbuf_v = last_vevac

                        for j in (range(8) if "pairs" in P1_PARTS else ()):
                            h1, h2 = 2 * j, 2 * j + 1
                            c.wait("t", "v", slab_ready_v)
                            c.wait("t", "v", state.get("smt_done", 0))
                            c.wait("t", "v", last_u_v)
                            c.wait("t", "s", last_readend_s)
                            c.wait("t", "s", tab_s)
                            c.wait("t", "g", tab_g)
                            rkc = (slab_col(slot, j, ch_cols[0]),
                                   slab_col(slot, j, ch_cols[1]))
                            wkc = (rkc[0] - 1, rkc[1] - 1)
                            c.inc(nc.tensor.matmul(
                                pSt[:, 0:128],
                                slab[0:64, wkc[0]:wkc[1]],
                                slab[0:64, rkc[0]:rkc[1]],
                                start=True, stop=True), "t")
                            tv = c.inc(nc.tensor.matmul(
                                pRd[:, 256:384],
                                slab[64:128, wkc[0]:wkc[1]],
                                slab[64:128, rkc[0]:rkc[1]],
                                start=True, stop=True, tile_position=(64, 0)), "t")
                            if "smt" not in P1_PARTS:
                                state["smt_done"] = c.n["v"]
                                continue
                            c.wait("v", "t", tv)
                            c.wait("v", "t", prev_rd_t)
                            smt_v = c.v(nc.vector.tensor_tensor(
                                smt_sb[:, 0:128], pSt[:, 0:128],
                                MT_tab[:, h1 * 128:(h1 + 1) * 128], ALU.mult))
                            smt_v = c.v(nc.vector.tensor_tensor(
                                smt_sb[:, 128:256], pRd[:, 256:384],
                                MT_tab[:, h2 * 128:(h2 + 1) * 128], ALU.mult))
                            rkg_v = c.v(nc.vector.tensor_tensor(
                                rkg_sb[:], slab[:, rkc[0]:rkc[1]],
                                gp2_tab[:, j * 128:(j + 1) * 128], ALU.mult))
                            state["smt_done"] = smt_v
                            c.wait("t", "v", rkg_v)
                            c.wait("t", "v", last_u_v)
                            c.wait("t", "v", vbuf_v)
                            c.wait("t", "s", last_readend_s)
                            if "rd" not in P1_PARTS:
                                continue
                            nc.tensor.matmul(pRd[0:64, 0:128],
                                             vbuf[:, h1 * 64:(h1 + 1) * 64],
                                             smt_sb[:, 0:128],
                                             start=True, stop=False)
                            nc.tensor.matmul(pRd[0:64, 0:128],
                                             U_tab[0:64, j * 64:(j + 1) * 64],
                                             rkg_sb[0:64, :],
                                             start=False, stop=True)
                            nc.tensor.matmul(pRd[64:128, 0:128],
                                             vbuf[:, h2 * 64:(h2 + 1) * 64],
                                             smt_sb[:, 128:256],
                                             start=True, stop=False,
                                             tile_position=(0, 64))
                            tv = c.inc(nc.tensor.matmul(
                                pRd[64:128, 0:128],
                                U_tab[64:128, j * 64:(j + 1) * 64],
                                rkg_sb[64:128, :],
                                start=False, stop=True,
                                tile_position=(64, 64)), "t")
                            prev_rd_t = tv
                            c.wait("s", "t", tv)
                            last_readend_s = c.s(nc.scalar.mul(
                                reads[:, j * 512 + cl * 128:
                                      j * 512 + (cl + 1) * 128],
                                pRd[:, 0:128], alpha_pair[:, j:j + 1]))
                            if "u" not in P1_PARTS:
                                continue
                            c.wait("t", "v", wkg_v)
                            nc.tensor.matmul(pU[0:64, 256:320],
                                             wkg[:, j * 128:j * 128 + 64],
                                             vbuf[:, h1 * 64:(h1 + 1) * 64],
                                             start=True, stop=True)
                            tv = c.inc(nc.tensor.matmul(
                                pU[64:128, 256:320],
                                wkg[:, j * 128 + 64:(j + 1) * 128],
                                vbuf[:, h2 * 64:(h2 + 1) * 64],
                                start=True, stop=True,
                                tile_position=(0, 64)), "t")
                            c.wait("v", "t", tv)
                            last_u_v = c.v(nc.vector.scalar_tensor_tensor(
                                out=U_tab[:, j * 64:(j + 1) * 64],
                                in0=U_tab[:, j * 64:(j + 1) * 64],
                                scalar=gC_pair[:, j:j + 1],
                                in1=pU[:, 256:320], op0=ALU.mult, op1=ALU.add))
                        last_pair_t = c.n["t"]

                    c.wait("t", "s", last_readend_s)
                    proj_evac = state.get("proj_evac", 0)
                    for ohalf in (range(2) if "proj" in P1_PARTS else ()):
                        for jc in range(8):
                            c.dma("qg", "wrs", wrt_s[:, jc * 512:(jc + 1) * 512],
                                  dram["WrT"][jc * 128:(jc + 1) * 128,
                                              ohalf * 512:(ohalf + 1) * 512],
                                  deps=[("t", state.get("wrs_free", 0))])
                        c.wait("t", "wrs", c.n["wrs"])
                        for oc in range(ohalf * 4, ohalf * 4 + 4):
                            oo = oc % 4
                            c.wait("t", "v", proj_evac)
                            for jc in range(8):
                                mm = nc.tensor.matmul(
                                    pPj[:],
                                    wrt_s[:, jc * 512 + oo * 128:
                                          jc * 512 + (oo + 1) * 128],
                                    reads[:, jc * 512:(jc + 1) * 512],
                                    start=(jc == 0), stop=(jc == 7))
                                tv = c.inc(mm, "t")
                            if oc % 4 == 3:
                                state["wrs_free"] = tv
                            c.wait("v", "t", tv)
                            c.v(nc.vector.tensor_add(
                                self.xn_tmp[:], pPj[:],
                                slab[:, slab_col(slot, oc, 1):slab_col(slot, oc, 513)]))
                            proj_evac = c.v(nc.vector.tensor_add(
                                self.xch(oc, sl), self.xch(oc, sl), self.xn_tmp[:]))
                    state["proj_evac"] = proj_evac
                    state["psum_free"] = c.n["v"]
                    st_cons["slab_t"][slot] = c.n["t"]
                    st_cons["slab_v"][slot] = c.n["v"]
            self.x_dep = {"v": c.n["v"]}

    def p2_ffn(self):
        c, nc = self.c, self.nc
        with ExitStack() as sc:
            xn = c.sbuf("xn2", [128, 8 * 512], BF16, scope=sc)
            bias_sb = c.sbuf("fbias", [128, 8], F32, scope=sc)
            p_s = c.psum("p_s", [128, 512], scope=sc)
            p_q = c.psum("p_q", [128, 512], scope=sc)
            py = [c.psum(f"py{i}", [128, 512], scope=sc) for i in range(2)]
            qb = c.dma("qs", "p2b", bias_sb[:, :], self.dram["fbias2d"][:, :])
            c.wait("v", "p2b", qb)
            state = {}
            slot_evac = [0, 0]
            prev_group_mm = 0
            for g in range(8):
                sl = slice(g * 512, (g + 1) * 512)
                rep_v = self.ln_group(g, p_s, p_q, state)
                c.wait("v", "t", prev_group_mm)
                xn_v = 0
                for dc in range(8):
                    xn_v = self.xn_chunk(xn[:, dc * 512:(dc + 1) * 512], dc, sl, rep_v)
                state["rep_free"] = xn_v
                c.wait("t", "v", xn_v)
                c.wait("t", "v", self.M_ready)
                for oc in range(8):
                    slot = oc % 2
                    pb = py[slot]
                    c.wait("t", "v", slot_evac[slot])
                    for jc in range(8):
                        mm = nc.tensor.matmul(
                            pb[:],
                            self.M_sb[:, jc * D + oc * 128: jc * D + (oc + 1) * 128],
                            xn[:, jc * 512:(jc + 1) * 512],
                            start=(jc == 0), stop=(jc == 7))
                    tv = c.inc(mm, "t")
                    c.wait("v", "t", tv)
                    slot_evac[slot] = c.v(nc.vector.scalar_tensor_tensor(
                        out=self.xch(oc, sl), in0=pb[:],
                        scalar=bias_sb[:, oc:oc + 1],
                        in1=self.xch(oc, sl), op0=ALU.add, op1=ALU.add))
                prev_group_mm = c.n["t"]
            self.x_dep = {"v": c.n["v"]}
        c.wait("q", "t", c.n["t"])
        self.m_scope.close()

    def p3_attn(self):
        c, nc = self.c, self.nc
        dram = self.dram
        xs1, xs2 = self.scr["xs1"], self.scr["xs2"]
        xnscr, cumpad = self.scr["xnscr"], self.scr["cumpad"]

        with ExitStack() as sc:
            xn_sp = c.sbuf("xn_sp", [128, 8 * 512], BF16, scope=sc)
            p_s = c.psum("p_s3", [128, 512], scope=sc)
            p_q = c.psum("p_q3", [128, 512], scope=sc)
            state = {}
            st_q = 0
            for g in range(8):
                sl = slice(g * 512, (g + 1) * 512)
                rep_v = self.ln_group(g, p_s, p_q, state)
                c.wait("v", "xnw", st_q)
                xn_v = 0
                for dc in range(8):
                    xn_v = self.xn_chunk(xn_sp[:, dc * 512:(dc + 1) * 512],
                                         dc, sl, rep_v)
                state["rep_free"] = xn_v
                c.wait("q", "v", xn_v)
                for dc in range(8):
                    c.dma("qs", "xnw", xnscr[dc * 128:(dc + 1) * 128, sl],
                          xn_sp[:, dc * 512:(dc + 1) * 512])
                    st_q = c.n["xnw"]
            self.wait_x("q")
            for dc in range(8):
                c.dma("qs", "xsp", xs1[dc * 128:(dc + 1) * 128, :], self.xch(dc))
            c.wait("q", "xsp", c.n["xsp"])
            c.wait("q", "xnw", c.n["xnw"])
        c.barrier()
        self.x_scope.close()

        if P3_DEBUG == "cum":
            self.p3_cum_build()
            with ExitStack() as sc:
                tb = c.sbuf("dbgc", [16, 4096], F32, scope=sc)
                q1 = c.dma("qs", "dbg", tb[:, :], self.scr["cumpad"][:, :],
                           deps=[("cpw", c.n.get("cpw", 0))])
                c.wait("q", "dbg", q1)
                c.dma("qs", "so", self.out[0:16, 0:4096], tb[:])
            c.wait("q", "so", c.n["so"])
            self.x_dep = {"so": c.n["so"]}
            return
        if P3_DEBUG == "xn":
            with ExitStack() as sc:
                tb = c.sbuf("dbg", [128, 4096], F32, scope=sc)
                for dc in range(8):
                    q1 = c.dma("qg", "dbg", tb[:, :],
                               self.scr["xnscr"][dc * 128:(dc + 1) * 128, :],
                               deps=[("so", c.n.get("so", 0))])
                    c.wait("q", "dbg", q1)
                    c.dma("qs", "so", self.out[dc * 128:(dc + 1) * 128, :], tb[:])
                    c.wait("g", "so", c.n["so"])
            c.wait("q", "so", c.n["so"])
            self.x_dep = {"so": c.n["so"]}
            return
        self.p3_cum_build()
        c.barrier()
        self.p3_main()

    def p3_cum_build(self):
        c, nc = self.c, self.nc
        dram = self.dram
        cumpad = self.scr["cumpad"]
        with ExitStack() as sc:
            rpet = c.sbuf("rpet", [128, 16 * 16], F32, scope=sc)
            ones5 = c.sbuf("ones5", [128, 512], F32, scope=sc)
            ltri = c.sbuf("ltri", [128, 512], F32, scope=sc)
            cum_sb = c.sbuf("cum_sb", [16, 2048], F32, scope=sc)
            padt = c.sbuf("padt", [16, 2048], F32, scope=sc)
            p_c = c.psum("p_c", [16, 512], scope=sc)
            q1 = 0
            for uc in range(16):
                q1 = c.dma("qs", "rpe", rpet[:, uc * 16:(uc + 1) * 16],
                           dram["rpeT"][uc * 128:(uc + 1) * 128, :])
            gvp = c.g(nc.gpsimd.memset(padt[:], PAD))
            gvo = c.g(nc.gpsimd.memset(ones5[:], 1.0))
            c.wait("t", "rpe", q1)
            ev = 0
            for mg in range(4):
                for uc in range(16):
                    c.wait("g", "g", gvo)
                    c.wait("g", "t", c.n["t"])
                    gv = c.g(nc.gpsimd.affine_select(
                        out=ltri[:], in_=ones5[:], compare_op=ALU.is_ge,
                        fill=0.0, base=mg * 512 - uc * 128,
                        pattern=[[1, 512]], channel_multiplier=-1))
                    c.wait("t", "g", gv)
                    c.wait("t", "v", ev)
                    tv = c.inc(nc.tensor.matmul(
                        p_c[:], rpet[:, uc * 16:(uc + 1) * 16], ltri[:],
                        start=(uc == 0), stop=(uc == 15)), "t")
                c.wait("v", "t", tv)
                ev = c.v(nc.vector.tensor_copy(
                    cum_sb[:, mg * 512:(mg + 1) * 512], p_c[:]))
            c.wait("q", "g", gvp)
            c.wait("q", "v", ev)
            c.dma("qs", "cpw", cumpad[:, 0:2048], padt[:, :])
            c.dma("qs", "cpw", cumpad[:, 2048:4096], cum_sb[:, :])
            c.wait("q", "cpw", c.n["cpw"])

    def p3_main(self):
        c, nc = self.c, self.nc
        dram = self.dram
        xs1, xs2, xnscr, cumpad = (self.scr["xs1"], self.scr["xs2"],
                                   self.scr["xnscr"], self.scr["cumpad"])
        with ExitStack() as sc:
            qkw_sb = c.sbuf("qkw_sb", [128, 8 * 2048], BF16, scope=sc)
            vw_sb = c.sbuf("vw_sb", [128, 8 * 1024], BF16, scope=sc)
            ow_sb = c.sbuf("ow_sb", [128, 8 * 1024], BF16, scope=sc)
            xn_b = c.sbuf("xn_b", [128, 8 * 2048], BF16, scope=sc)
            QK = c.sbuf("QK", [64, 2 * 2048], BF16, scope=sc)
            Vt = c.sbuf("Vt", [128, 16 * 1024], BF16, scope=sc)
            mix = c.sbuf("mix", [128, 8 * 2048], BF16, scope=sc)
            Hs = c.sbuf("Hs", [128, 2816], BF16, scope=sc)
            Ssb = c.sbuf("Ssb", [128, 2048], F32, scope=sc)
            Psb = c.sbuf("Psb", [128, 2048], BF16, scope=sc)
            PTsb = c.sbuf("PTsb", [128, 16 * 128], BF16, scope=sc)
            diag = c.sbuf("diag3", [128, 128], BF16, scope=sc)
            mrow = c.sbuf("mrow", [128, 4], F32, scope=sc)
            xld = c.sbuf("xld", [128, 512], F32, scope=sc)
            xout = c.sbuf("xout", [128, 512], F32, scope=sc)
            pQK = c.psum("pQK", [128, 512], scope=sc)
            pV2 = c.psum("pV2", [128, 512], scope=sc)
            pS = c.psum("pS", [128, 512], scope=sc)
            pPT = c.psum("pPT", [128, 512], scope=sc)
            pPT2 = c.psum("pPT2", [128, 512], scope=sc)
            pMix = c.psum("pMix", [128, 512], scope=sc)
            pY = c.psum("pY", [128, 512], scope=sc)

            for dc in range(8):
                c.dma("qg", "w3", qkw_sb[:, dc * 2048:(dc + 1) * 2048],
                      dram["qkw"][dc * 128:(dc + 1) * 128, :])
                c.dma("qg", "w3", vw_sb[:, dc * 1024:(dc + 1) * 1024],
                      dram["vw"][dc * 128:(dc + 1) * 128, :])
                c.dma("qg", "w3", ow_sb[:, dc * 1024:(dc + 1) * 1024],
                      dram["ow"][dc * 128:(dc + 1) * 128, :])
            qw3 = c.n["w3"]
            state = {}
            for b in range(B_LOC):
                qx = 0
                for dc in range(8):
                    qx = c.dma("qs", "xn3",
                               xn_b[:, dc * 2048:(dc + 1) * 2048],
                               xnscr[dc * 128:(dc + 1) * 128, b * T:(b + 1) * T],
                               deps=[("t", state.get("xnb_f", 0))])
                c.wait("t", "xn3", qx)
                c.wait("t", "w3", qw3)

                def xg(dc, g):
                    return xn_b[:, dc * 2048 + g * 512: dc * 2048 + (g + 1) * 512]

                for g in range(4):
                    for tt in range(4):
                        ti = g * 4 + tt
                        for half in range(2):
                            c.wait("t", "v", state.get("pv2_f", 0))
                            for dc in range(8):
                                mm = nc.tensor.matmul(
                                    pV2[:],
                                    xg(dc, g)[:, tt * 128:(tt + 1) * 128],
                                    vw_sb[:, dc * 1024 + half * 512:
                                          dc * 1024 + (half + 1) * 512],
                                    start=(dc == 0), stop=(dc == 7))
                            tv = c.inc(mm, "t")
                            c.wait("v", "t", tv)
                            state["pv2_f"] = c.v(nc.vector.tensor_copy(
                                Vt[:, ti * 1024 + half * 512:
                                   ti * 1024 + (half + 1) * 512], pV2[:]))

                for h in range(16):
                    for qk in range(2):
                        for g in range(4):
                            c.wait("t", "v", state.get("pqk_f", 0))
                            for dc in range(8):
                                mm = nc.tensor.matmul(
                                    pQK[0:64, :],
                                    qkw_sb[:, dc * 2048 + h * 128 + qk * 64:
                                           dc * 2048 + h * 128 + qk * 64 + 64],
                                    xg(dc, g), start=(dc == 0), stop=(dc == 7))
                            tv = c.inc(mm, "t")
                            c.wait("v", "t", tv)
                            c.wait("v", "t", state.get("qk_cons", 0))
                            state["pqk_f"] = c.v(nc.vector.tensor_copy(
                                QK[:, qk * 2048 + g * 512:
                                   qk * 2048 + (g + 1) * 512], pQK[0:64, :]))
                    qk_v = c.n["v"]
                    c.wait("t", "v", qk_v)
                    if P3_BARRIER >= 1:
                        c.barrier()
                    qh = c.dma("qg", "hs", Hs[:, :],
                               _AP(tensor=cumpad.tensor, offset=h * 4096 + 1153,
                                   ap=[[1, 128], [1, 2816]]),
                               deps=[("v", state.get("hs_f", 0)),
                                     ("s", state.get("hs_fs", 0))])
                    c.wait("v", "hs", qh)
                    c.wait("s", "hs", qh)
                    for qt in range(16):
                        if P3_BARRIER >= 2:
                            c.barrier()
                        nk = (qt // 4 + 1) * 512
                        for kg in range(nk // 512):
                            c.wait("v", "s", state.get("hs_fs", 0))
                            c.wait("t", "v", state.get("pS_f", 0))
                            tv = c.inc(nc.tensor.matmul(
                                pS[:],
                                QK[0:64, qt * 128:(qt + 1) * 128],
                                QK[0:64, 2048 + kg * 512:2048 + (kg + 1) * 512],
                                start=True, stop=True, tile_position=(0, 0)), "t")
                            m_off = qt - 4 * kg
                            colbase = 128 * (m_off + 3)
                            c.wait("v", "t", tv)
                            hrev = Hs[:, colbase + 511:colbase - 1:-1] \
                                if colbase > 0 else Hs[:, colbase + 511::-1]
                            state["pS_f"] = c.v(nc.vector.scalar_tensor_tensor(
                                out=Ssb[:, kg * 512:(kg + 1) * 512], in0=pS[:],
                                scalar=self.consts[:, 1:2], in1=hrev,
                                op0=ALU.mult, op1=ALU.add))
                        state["hs_f"] = state["pS_f"]
                        if P3_DEBUG == "soft" and b == 0 and h == 0 and qt == 0:
                            state["dump_soft"] = True
                        if P3_DEBUG == "ssb" and b == 0 and h == 0 and qt == 0:
                            c.wait("q", "v", state["pS_f"])
                            c.dma("qs", "so", self.out[0:128, 0:512],
                                  Ssb[:, 0:512])
                            hv = 0
                            for hc in range(2):
                                c.wait("v", "so", c.n["so"])
                                hv = c.v(nc.vector.tensor_copy(
                                    Ssb[:, 0:1408], Hs[:, hc * 1408:(hc + 1) * 1408]))
                                c.wait("q", "v", hv)
                                c.dma("qs", "so",
                                      self.out[0:128, 512 + hc * 1408:
                                               512 + (hc + 1) * 1408],
                                      Ssb[:, 0:1408])
                            c.wait("q", "so", c.n["so"])
                            self.x_dep = {"so": c.n["so"]}
                            return
                        c.wait("v", "s", state.get("hs_fs", 0))
                        vv = c.v(nc.vector.reduce_max(
                            mrow[:, 0:1], Ssb[:, 0:nk], axis=mybir.AxisListType.X))
                        c.wait("v", "v", vv)
                        vv = c.v(nc.vector.tensor_scalar_mul(
                            mrow[:, 1:2], mrow[:, 0:1], -1.0))
                        c.wait("s", "v", vv)
                        c.wait("s", "t", state.get("Psb_f", 0))
                        sv = c.s(nc.scalar.activation(
                            Psb[:, 0:nk], Ssb[:, 0:nk], AF.Exp,
                            bias=mrow[:, 1:2], scale=1.0,
                            accum_out=mrow[:, 2:3]))
                        state["hs_fs"] = sv
                        c.wait("v", "s", sv)
                        vv = c.v(nc.vector.reciprocal(mrow[:, 3:4], mrow[:, 2:3]))
                        c.wait("v", "v", vv)
                        c.wait("v", "t", state.get("diag_f", 0))
                        dv = c.v(nc.vector.tensor_scalar_mul(
                            diag[:], self.eye128[:], mrow[:, 3:4]))
                        if state.get("dump_soft"):
                            state["dump_soft"] = False
                            c.wait("v", "s", sv)
                            pv32 = c.v(nc.vector.tensor_copy(Ssb[:, 0:512],
                                                             Psb[:, 0:512]))
                            c.wait("q", "v", pv32)
                            c.dma("qs", "so", self.out[0:128, 0:512], Ssb[:, 0:512])
                            dv0 = c.v(nc.vector.tensor_scalar_mul(
                                diag[:], self.eye128[:], mrow[:, 3:4]))
                            c.wait("v", "so", c.n["so"])
                            dv32 = c.v(nc.vector.tensor_copy(Ssb[:, 0:128], diag[:]))
                            c.wait("q", "v", dv32)
                            c.dma("qs", "so", self.out[0:128, 512:640], Ssb[:, 0:128])
                            c.wait("q", "v", c.n["v"])
                            c.dma("qs", "so", self.out[0:128, 640:644], mrow[:, 0:4])
                            c.wait("q", "so", c.n["so"])
                            self.x_dep = {"so": c.n["so"]}
                            return
                        for kc in range(nk // 128):
                            pt_bank = pPT if kc % 2 == 0 else pPT2
                            c.wait("t", "v", dv)
                            c.wait("t", "s", sv)
                            c.wait("t", "v", state.get(("pt_f", kc % 2), 0))
                            tv = c.inc(nc.tensor.matmul(
                                pt_bank[:, 0:128],
                                Psb[:, kc * 128:(kc + 1) * 128], diag[:],
                                start=True, stop=True), "t")
                            c.wait("v", "t", tv)
                            state[("pt_f", kc % 2)] = c.v(nc.vector.tensor_copy(
                                PTsb[:, kc * 128:(kc + 1) * 128],
                                pt_bank[:, 0:128]))
                        state["diag_f"] = c.n["t"]
                        state["Psb_f"] = c.n["t"]
                        ptv = c.n["v"]
                        c.wait("t", "v", ptv)
                        c.wait("t", "v", state.get("pmix_f", 0))
                        nkc2 = nk // 128
                        for kc in range(nkc2):
                            mm = nc.tensor.matmul(
                                pMix[0:64, 0:128] if h % 2 == 0
                                else pMix[64:128, 0:128],
                                Vt[:, kc * 1024 + (h // 2) * 128 + (h % 2) * 64:
                                   kc * 1024 + (h // 2) * 128 + (h % 2) * 64 + 64],
                                PTsb[:, kc * 128:(kc + 1) * 128],
                                start=(kc == 0), stop=(kc == nkc2 - 1),
                                **({} if h % 2 == 0 else
                                   dict(tile_position=(0, 64))))
                        tv = c.inc(mm, "t")
                        c.wait("v", "t", tv)
                        ro = slice(0, 64) if h % 2 == 0 else slice(64, 128)
                        state["pmix_f"] = c.v(nc.vector.tensor_copy(
                            mix[ro, (h // 2) * 2048 + qt * 128:
                                (h // 2) * 2048 + (qt + 1) * 128],
                            pMix[ro, 0:128]))
                        if (P3_DEBUG == "mix" and b == 0 and h == 1
                                and qt == 0):
                            c.wait("v", "v", state["pmix_f"])
                            dm = c.v(nc.vector.tensor_copy(Ssb[:, 1792:1796],
                                                           mrow[:, 0:4]))
                            dd = c.v(nc.vector.tensor_copy(Ssb[:, 1536:1664],
                                                           diag[:]))
                            c.wait("q", "v", dd)
                            c.dma("qs", "so", self.out[0:128, 1792:1796],
                                  Ssb[:, 1792:1796])
                            c.dma("qs", "so", self.out[0:128, 1796:1924],
                                  Ssb[:, 1536:1664])
                            d1 = c.v(nc.vector.tensor_copy(
                                Ssb[:, 0:512], PTsb[:, 0:512]))
                            c.wait("q", "v", d1)
                            c.dma("qs", "so", self.out[0:128, 0:512],
                                  Ssb[:, 0:512])
                            c.wait("v", "so", c.n["so"])
                            d2 = c.v(nc.vector.tensor_copy(
                                Ssb[:, 0:256], mix[:, 0:256]))
                            c.wait("q", "v", d2)
                            c.dma("qs", "so", self.out[0:128, 512:768],
                                  Ssb[:, 0:256])
                            c.wait("v", "so", c.n["so"])
                            d3 = c.v(nc.vector.tensor_copy(
                                Ssb[:, 0:1024], Vt[:, 0:1024]))
                            c.wait("q", "v", d3)
                            c.dma("qs", "so", self.out[0:128, 768:1792],
                                  Ssb[:, 0:1024])
                            c.wait("q", "so", c.n["so"])
                            self.x_dep = {"so": c.n["so"]}
                            return
                    state["qk_cons"] = c.n["t"]

                c.wait("t", "v", c.n["v"])
                for g in range(4):
                    sl_abs = slice(b * T + g * 512, b * T + (g + 1) * 512)
                    for oc in range(8):
                        xslot = 0
                        qx = c.dma("qs", "xr0", xld[:, 0:512],
                                   xs1[oc * 128:(oc + 1) * 128, sl_abs],
                                   deps=[("v", state.get(("xld", 0), 0))])
                        c.wait("t", "v", state.get("pY_f", 0))
                        for jc in range(8):
                            mm = nc.tensor.matmul(
                                pY[:],
                                ow_sb[:, jc * 1024 + oc * 128:
                                      jc * 1024 + (oc + 1) * 128],
                                mix[:, jc * 2048 + g * 512:
                                    jc * 2048 + (g + 1) * 512],
                                start=(jc == 0), stop=(jc == 7))
                        tv = c.inc(mm, "t")
                        c.wait("v", "t", tv)
                        c.wait("v", "xr0", qx)
                        c.wait("v", "xo0", state.get(("xout", 0), 0))
                        pY_f = c.v(nc.vector.tensor_add(
                            xout[:, 0:512], pY[:], xld[:, 0:512]))
                        state["pY_f"] = pY_f
                        state[("xld", xslot)] = pY_f
                        c.wait("q", "v", pY_f)
                        c.dma("qs", "xo0",
                              xs2[oc * 128:(oc + 1) * 128, sl_abs],
                              xout[:, 0:512])
                        state[("xout", 0)] = c.n["xo0"]
                state["xnb_f"] = c.n["t"]
            c.wait("q", "xo0", c.n["xo0"])
        self.x_dep = {"xo0": c.n["xo0"]}

    def p4_moe(self):
        c, nc = self.c, self.nc
        dram = self.dram
        if "p3" in self.phases:
            xs2 = self.scr["xs2"]
        else:
            xs2 = dram["x"]
            self.wait_x("q")
            c.barrier()
            self.x_scope.close()
        G = 1024
        NG = TOK // G
        with ExitStack() as sc:
            xg = c.sbuf("xg4", [128, 8 * G], F32, scope=sc)
            xn4 = c.sbuf("xn4", [128, 8 * G], BF16, scope=sc)
            hbuf = c.sbuf("hbuf", [128, 32 * G], BF16, scope=sc)
            oacc = c.sbuf("oacc", [128, 8 * G], F32, scope=sc)
            w1b = c.sbuf("w1b", [128, 2 * 8 * 128], BF16, scope=sc)
            w2b = c.sbuf("w2b", [128, 2 * 32 * 128], BF16, scope=sc)
            cwr = c.sbuf("cwr", [128, G], BF16, scope=sc)
            cw9 = c.sbuf("cw9", [9, G], BF16, scope=sc)
            wg_sb = c.sbuf("wg_sb", [128, 8 * 8], BF16, scope=sc)
            b2s_sb = c.sbuf("b2s_sb", [9, D], BF16, scope=sc)
            b1_sb = c.sbuf("b1_sb", [128, NE * 32], F32, scope=sc)
            bs1_sb = c.sbuf("bs1_sb", [128, 32], F32, scope=sc)
            sel8 = c.sbuf("sel8", [8, 8 * 128], BF16, scope=sc)
            grow4 = c.sbuf("grow4", [128, 16], F32, scope=sc)
            xout4 = c.sbuf("xout4", [128, 512], F32, scope=sc)
            p_s = c.psum("p_s4", [128, 512], scope=sc)
            p_q = c.psum("p_q4", [128, 512], scope=sc)
            pH1 = c.psum("pH1", [128, 512], scope=sc)
            pH2 = c.psum("pH2", [128, 512], scope=sc)
            pO1 = c.psum("pO1", [128, 512], scope=sc)
            pO2 = c.psum("pO2", [128, 512], scope=sc)
            pG = c.psum("pG", [128, 512], scope=sc)

            for dc in range(8):
                c.dma("qg", "w4s", wg_sb[:, dc * 8:(dc + 1) * 8],
                      dram["Wg"][dc * 128:(dc + 1) * 128, :])
            c.dma("qg", "w4s", b2s_sb[:, :], dram["b2s"][:, :])
            c.dma("qs", "w4f", b1_sb[:, :], dram["b1_2d"][:, :])
            c.dma("qs", "w4f", bs1_sb[:, :], dram["bs1_2d"][:, :])
            qconst = c.dma("qg", "w4s", cw9[8:9, :],
                           dram["c01"][0:1, 0:G])
            gv = c.g(nc.gpsimd.memset(sel8[:], 1.0))
            c.wait("g", "g", gv)
            selg = 0
            for e in range(NE):
                selg = c.g(nc.gpsimd.affine_select(
                    out=sel8[:, e * 128:(e + 1) * 128],
                    in_=sel8[:, e * 128:(e + 1) * 128],
                    compare_op=ALU.is_equal, fill=0.0,
                    base=-e, pattern=[[0, 128]], channel_multiplier=1))
            state = {}
            for g4 in range(NG):
                gsl = slice(g4 * G, (g4 + 1) * G)
                qx = 0
                for dc in range(8):
                    qx = c.dma("qs", "x4", xg[:, dc * G:(dc + 1) * G],
                               xs2[dc * 128:(dc + 1) * 128, gsl],
                               deps=[("v", state.get("xg_f", 0)),
                                     ("s", state.get("xg_fs", 0)),
                                     ("xo0", self.x_dep.get("xo0", 0))])
                c.wait("t", "x4", qx)
                c.wait("s", "x4", qx)
                c.wait("v", "x4", qx)
                xn_v = 0
                for sub in range(2):
                    xsrc = lambda dc, s=sub: xg[:, dc * G + s * 512:
                                                dc * G + (s + 1) * 512]
                    rep_v = self.ln_group(g4 * 2 + sub, p_s, p_q, state, xsrc=xsrc)
                    c.wait("v", "t", state.get("xn_cons", 0))
                    for dc in range(8):
                        xn_v = self.xn_chunk(
                            xn4[:, dc * G + sub * 512: dc * G + (sub + 1) * 512],
                            dc, slice(0, 512), rep_v, xsrc=xsrc)
                    state["rep_free"] = xn_v
                c.wait("t", "v", xn_v)
                c.wait("t", "w4s", c.n["w4s"])
                c.wait("t", "g", selg)
                _ss_save = c.self_sync
                c.self_sync = True
                for tt in range(8):
                    c.wait("t", "v", state.get("pg_f", 0))
                    c.wait("t", "s", state.get("pg_fs", 0))
                    for dc in range(8):
                        mm = nc.tensor.matmul(
                            pG[0:128, 0:8],
                            xn4[:, dc * G + tt * 128: dc * G + (tt + 1) * 128],
                            wg_sb[:, dc * 8:(dc + 1) * 8],
                            start=(dc == 0), stop=(dc == 7))
                    tv = c.inc(mm, "t")
                    c.wait("v", "t", tv)
                    vv = c.v(nc.vector.reduce_max(grow4[:, 0:1], pG[:, 0:8],
                                                  axis=mybir.AxisListType.X))
                    c.wait("v", "v", vv)
                    vv = c.v(nc.vector.tensor_scalar_mul(grow4[:, 1:2],
                                                         grow4[:, 0:1], -1.0))
                    c.wait("s", "v", vv)
                    sv = c.s(nc.scalar.activation(
                        grow4[:, 8:16], pG[:, 0:8], AF.Exp,
                        bias=grow4[:, 1:2], scale=1.0,
                        accum_out=grow4[:, 2:3]))
                    state["pg_fs"] = sv
                    qrow = grow4[:, 8:16]
                    c.wait("v", "s", sv)
                    vv = c.v(nc.vector.reduce_max(grow4[:, 3:4], qrow,
                                                  axis=mybir.AxisListType.X))
                    c.wait("v", "v", vv)
                    c.v(nc.vector.tensor_scalar(
                        out=grow4[:, 16:24] if False else self.ln_tmp[:, 0:8],
                        in0=qrow, scalar1=grow4[:, 3:4], scalar2=None,
                        op0=ALU.is_equal))
                    c.v(nc.vector.tensor_scalar(
                        out=self.ln_tmp[:, 8:16], in0=self.ln_tmp[:, 0:8],
                        scalar1=-1.0, scalar2=1.0, op0=ALU.mult, op1=ALU.add))
                    vv = c.v(nc.vector.tensor_mul(self.ln_tmp[:, 16:24], qrow,
                                                  self.ln_tmp[:, 8:16]))
                    c.wait("v", "v", vv)
                    vv = c.v(nc.vector.reduce_max(grow4[:, 4:5],
                                                  self.ln_tmp[:, 16:24],
                                                  axis=mybir.AxisListType.X))
                    c.wait("v", "v", vv)
                    c.v(nc.vector.tensor_add(grow4[:, 5:6], grow4[:, 3:4],
                                             grow4[:, 4:5]))
                    vv = c.v(nc.vector.scalar_tensor_tensor(
                        out=grow4[:, 5:6], in0=grow4[:, 2:3], scalar=1e-9,
                        in1=grow4[:, 5:6], op0=ALU.mult, op1=ALU.add))
                    c.wait("v", "v", vv)
                    vv = c.v(nc.vector.reciprocal(grow4[:, 6:7], grow4[:, 5:6]))
                    c.v(nc.vector.tensor_scalar(
                        out=self.ln_tmp[:, 0:8], in0=qrow,
                        scalar1=grow4[:, 4:5], scalar2=None, op0=ALU.is_ge))
                    vv2 = c.v(nc.vector.tensor_mul(self.ln_tmp[:, 8:16], qrow,
                                                   self.ln_tmp[:, 0:8]))
                    c.wait("v", "v", vv)
                    vv = c.v(nc.vector.tensor_scalar_mul(
                        self.ln_tmp[:, 0:8], self.ln_tmp[:, 8:16],
                        grow4[:, 6:7]))
                    c.wait("t", "v", vv)
                    c.wait("t", "s", state["pg_fs"])
                    c.wait("t", "g", self.eye_ready)
                    tv = c.inc(nc.tensor.transpose(
                        pG[0:8, 128:256], self.ln_tmp[:, 0:8],
                        self.eye128[:]), "t")
                    c.wait("v", "t", tv)
                    c.wait("v", "w4s", qconst)
                    state["pg_f"] = c.v(nc.vector.tensor_copy(
                        cw9[0:8, tt * 128:(tt + 1) * 128], pG[0:8, 128:256]))
                c.self_sync = _ss_save
                gate_v = c.n["v"]
                if P3_DEBUG == "cw" and g4 == 0:
                    c.wait("g", "v", gate_v)
                    c.dma("qg", "dbgw", self.out[0:9, 0:G], cw9[:, :])
                    c.wait("q", "v", gate_v)
                    c.dma("qs", "dbgw2", self.out[16:144, 1024:1040],
                          grow4[:, 0:16])
                    c.wait("q", "dbgw", c.n["dbgw"])
                    c.wait("q", "dbgw2", c.n["dbgw2"])
                    self.x_dep = {"dbgw": c.n["dbgw"]}
                    return

                first_acc = True
                for e in range(NE + 1):
                    shared = (e == NE)
                    self.p4_expert(g4, e, shared, xn4, hbuf, oacc, w1b, w2b,
                                   cwr, cw9, sel8, b1_sb, bs1_sb, b2s_sb,
                                   pH1, pH2, pO1, pO2, pG, state, first_acc)
                    first_acc = False

                for dc in range(8):
                    for half in range(2):
                        c.wait("v", "v", state.get("oacc_done", 0))
                        c.wait("v", "o4", state.get("xout4_f", 0))
                        ov = c.v(nc.vector.tensor_add(
                            xout4[:, 0:512],
                            oacc[:, dc * G + half * 512: dc * G + (half + 1) * 512],
                            xg[:, dc * G + half * 512: dc * G + (half + 1) * 512]))
                        c.wait("q", "v", ov)
                        c.dma("qs", "o4",
                              self.out[dc * 128:(dc + 1) * 128,
                                       g4 * G + half * 512: g4 * G + (half + 1) * 512],
                              xout4[:, 0:512])
                        state["xout4_f"] = c.n["o4"]
                state["xg_f"] = c.n["v"]
                state["xg_fs"] = c.n["s"]
                state["xn_cons"] = c.n["t"]
                state["stat_v"] = c.n["v"]
            c.wait("q", "o4", c.n["o4"])

    def p4_expert(self, g4, e, shared, xn4, hbuf, oacc, w1b, w2b, cwr, cw9,
                  sel8, b1_sb, bs1_sb, b2s_sb, pH1, pH2, pO1, pO2, pG, state,
                  first_acc):
        c, nc = self.c, self.nc
        dram = self.dram
        G = 1024
        if not shared:
            for half in range(2):
                c.wait("t", "v", state.get("cwr_f", 0))
                c.wait("t", "v", state.get("pg_f", 0))
                tv = c.inc(nc.tensor.matmul(
                    pG[:, 0:512],
                    sel8[:, e * 128:(e + 1) * 128],
                    cw9[0:8, half * 512:(half + 1) * 512],
                    start=True, stop=True), "t")
                c.wait("v", "t", tv)
                state["cwr_f"] = c.v(nc.vector.tensor_copy(
                    cwr[:, half * 512:(half + 1) * 512], pG[:, 0:512]))
        cw_v = state.get("cwr_f", 0)
        def w1_load(fc):
            slot = fc % 2
            base = dram["Ws1"] if shared else dram["W1"]
            off = (fc * 128 if shared
                   else e * 1024 * 4096 + fc * 128)
            return c.dma("qg", f"w1{slot}",
                         w1b[:, slot * 1024:(slot + 1) * 1024]
                         .rearrange("p (a f) -> p a f", a=8),
                         _AP(tensor=base.tensor, offset=off,
                             ap=[[4096, 128], [128 * 4096, 8], [1, 128]]),
                         deps=[("t", state.get(("w1f", slot), 0))])

        w1q = [0, 0]
        w1q[0] = w1_load(0)
        for fc in range(32):
            slot = fc % 2
            if fc + 1 < 32:
                w1q[(fc + 1) % 2] = w1_load(fc + 1)
            c.wait("t", f"w1{slot}", w1q[slot])
            for half, pH in ((0, pH1), (1, pH2)):
                c.wait("t", "s", state.get(("ph_f", half), 0))
                for dc in range(8):
                    mm = nc.tensor.matmul(
                        pH[:],
                        w1b[:, slot * 1024 + dc * 128: slot * 1024 + (dc + 1) * 128],
                        xn4[:, dc * G + half * 512: dc * G + (half + 1) * 512],
                        start=(dc == 0), stop=(dc == 7))
                tv = c.inc(mm, "t")
                state[("w1f", slot)] = tv
                c.wait("s", "t", tv)
                c.wait("s", "t", state.get(("hb_cons", fc, half), 0))
                c.wait("s", "w4f", 32)
                bias_ap = (bs1_sb[:, fc:fc + 1] if shared
                           else b1_sb[:, e * 32 + fc: e * 32 + fc + 1])
                sv = c.s(nc.scalar.activation(
                    hbuf[:, fc * G + half * 512: fc * G + (half + 1) * 512],
                    pH[:], AF.Silu, bias=bias_ap, scale=1.0))
                state[("ph_f", half)] = sv
                c.wait("v", "s", sv)
                c.wait("v", "v", cw_v)
                if shared:
                    vv = c.v(nc.vector.tensor_scalar_mul(
                        hbuf[:, fc * G + half * 512: fc * G + (half + 1) * 512],
                        hbuf[:, fc * G + half * 512: fc * G + (half + 1) * 512],
                        SHARED_SCALE))
                else:
                    vv = c.v(nc.vector.tensor_mul(
                        hbuf[:, fc * G + half * 512: fc * G + (half + 1) * 512],
                        hbuf[:, fc * G + half * 512: fc * G + (half + 1) * 512],
                        cwr[:, half * 512:(half + 1) * 512]))
                state[("hb", fc, half)] = vv
        def w2_load(dc):
            slot = dc % 2
            base = dram["Ws2"] if shared else dram["W2"]
            off = (dc * 128 if shared
                   else e * 4096 * 1024 + dc * 128)
            return c.dma("qg", f"w2{slot}",
                         w2b[:, slot * 4096:(slot + 1) * 4096]
                         .rearrange("p (a f) -> p a f", a=32),
                         _AP(tensor=base.tensor, offset=off,
                             ap=[[1024, 128], [128 * 1024, 32], [1, 128]]),
                         deps=[("t", state.get(("w2f", slot), 0))])

        w2q = [0, 0]
        w2q[0] = w2_load(0)
        for dc in range(8):
            slot = dc % 2
            if dc + 1 < 8:
                w2q[(dc + 1) % 2] = w2_load(dc + 1)
            c.wait("t", f"w2{slot}", w2q[slot])
            for half, pO in ((0, pO1), (1, pO2)):
                c.wait("t", "v", state.get(("po_f", half), 0))
                for fc in range(32):
                    c.wait("t", "v", state.get(("hb", fc, half), 0))
                    mm = nc.tensor.matmul(
                        pO[:],
                        w2b[:, slot * 4096 + fc * 128: slot * 4096 + (fc + 1) * 128],
                        hbuf[:, fc * G + half * 512: fc * G + (half + 1) * 512],
                        start=(fc == 0), stop=(fc == 31 and not shared))
                if shared:
                    mm = nc.tensor.matmul(
                        pO[:], b2s_sb[:, dc * 128:(dc + 1) * 128],
                        cw9[:, half * 512:(half + 1) * 512],
                        start=False, stop=True)
                tv = c.inc(mm, "t")
                state[("w2f", slot)] = tv
                c.wait("v", "t", tv)
                dst = oacc[:, dc * G + half * 512: dc * G + (half + 1) * 512]
                if first_acc:
                    vv = c.v(nc.vector.tensor_copy(dst, pO[:]))
                else:
                    vv = c.v(nc.vector.tensor_add(dst, dst, pO[:]))
                state[("po_f", half)] = vv
                if shared:
                    state["oacc_done"] = vv
        for fc in range(32):
            for half in range(2):
                state[("hb_cons", fc, half)] = c.n["t"]


def make_dft_tables():
    jk = np.outer(np.arange(D, dtype=np.int64), np.arange(D, dtype=np.int64)) % D
    th = 2.0 * np.pi * jk.astype(np.float64) / D
    return np.cos(th).astype(np.float32), np.sin(th).astype(np.float32)


def stage_inputs(inputs):
    Ctab, Stab = make_dft_tables()
    x = inputs["L0_attn_out"]
    qkv = inputs["L1_attn_qkv"]
    qk_cols = []
    for h in range(NH):
        qk_cols.append(qkv[:, 0, h, :])
        qk_cols.append(qkv[:, 1, h, :])
    qkw = np.ascontiguousarray(np.concatenate(qk_cols, axis=1))
    vw = np.ascontiguousarray(
        np.concatenate([qkv[:, 2, h, :] for h in range(NH)], axis=1))
    shared = dict(
        WwT=np.ascontiguousarray(inputs["L0_attn_W_write"].T),
        WrT=np.ascontiguousarray(inputs["L0_attn_W_read"].T),
        decay=inputs["L0_attn_decay"],
        log_alpha=inputs["L0_attn_log_alpha"],
        Ctab=Ctab, Stab=Stab,
        wfr=inputs["L0_ffn_w_freq_real"],
        wfi=inputs["L0_ffn_w_freq_imag"],
        fbias2d=np.ascontiguousarray(inputs["L0_ffn_bias"].reshape(8, 128).T),
        qkw=qkw, vw=vw,
        ow=np.ascontiguousarray(inputs["L1_attn_out_w"].reshape(NH * HD, D)),
        rpeT=np.ascontiguousarray(inputs["L1_attn_rpe_bias"].T),
        Wg=inputs["L1_ffn_Wg"],
        W1=inputs["L1_ffn_W1"],
        b1_2d=np.ascontiguousarray(
            inputs["L1_ffn_b1"].reshape(NE, 32, 128).transpose(2, 0, 1)
            .reshape(128, NE * 32)),
        W2=inputs["L1_ffn_W2"],
        Ws1=inputs["L1_ffn_Ws1"],
        bs1_2d=np.ascontiguousarray(inputs["L1_ffn_bs1"].reshape(32, 128).T),
        Ws2=inputs["L1_ffn_Ws2"],
        b2s=np.ascontiguousarray(np.concatenate(
            [inputs["L1_ffn_b2"], inputs["L1_ffn_bs2"][None, :]], axis=0)),
        c01=np.full((1, 1024), 0.1, np.float32),
    )
    maps = []
    for ci in range(NCORES):
        xc = x[ci * B_LOC:(ci + 1) * B_LOC].reshape(TOK, D)
        m = dict(shared)
        m["x"] = np.ascontiguousarray(xc.T)
        maps.append(m)
    return maps


def kernel(**inputs):
    maps = stage_inputs(inputs)
    nc = build_program()
    res = run_bass_kernel_spmd(nc, maps, core_ids=list(range(NCORES)))
    outs = []
    for ci in range(NCORES):
        o = np.asarray(res.results[ci]["out"])
        outs.append(np.ascontiguousarray(o.T).reshape(B_LOC, T, D))
    return np.concatenate(outs, axis=0).astype(np.float32)
